# version 01
# baseline (speedup 1.0000x reference)
import numpy as np
import ml_dtypes
from contextlib import ExitStack

import concourse.bass as bass
import concourse.mybir as mybir
from concourse.bass_utils import run_bass_kernel_spmd

F32 = mybir.dt.float32
BF16 = mybir.dt.bfloat16
AF = mybir.ActivationFunctionType
ALU = mybir.AluOpType
AX = mybir.AxisListType

T = 2048
D = 1024
NTS = 16
NCH = 8
NH = 16
NE = 12
TAU = 1.5
TT = 1024
NTSH = 8
NG = 4
EPS = 1e-5


class Sem:
    def __init__(self, nc, ctx, name):
        self.h = ctx.enter_context(nc.semaphore(name))
        self.name = name
        self.n = 0

    def tok(self):
        return (self, self.n)


class Plan:
    ENGS = ("tensor", "vector", "scalar", "sync", "gpsimd")

    def __init__(self):
        self.ops = {e: [] for e in self.ENGS}
        self.seen = {e: {} for e in self.ENGS}

    def emit(self, eng, fn, waits=(), inc=None, inck=1):
        wl = []
        seen = self.seen[eng]
        for w in waits:
            if w is None:
                continue
            s, v = w
            if v <= 0 or seen.get(s.name, 0) >= v:
                continue
            assert v <= s.n, f"wait on future token {s.name}:{v}>{s.n}"
            seen[s.name] = v
            wl.append((s.h, v))
        if inc is not None:
            inc.n += inck

        def op(e, wl=wl, fn=fn, inc=inc, inck=inck):
            for h, v in wl:
                e.wait_ge(h, v)
            inst = fn(e)
            if inc is not None:
                inst.then_inc(inc.h, inck)
        self.ops[eng].append(op)
        return inc.tok() if inc is not None else None

    def drain(self, eng):
        self.ops[eng].append(lambda e: e.drain())


def ln_chain(P, x_tm, cols, scale_rep, shift_rep, h_stage, psums, id32,
             h_fm, hfm_tok, stat_v, stat_s, rep_tok, sm_t, sm_v, sm_s, epscol,
             rt_f32=None, wr_sb=None, logits_ps=None, upto=4):
    s1c, s2c, meanc, ex2c, nvarc, lnvc, rstdc = cols
    psA, psB = psums
    ps_free = [None, None]
    ps_free_s = [None, None]
    hst_free = [None, None]
    rtmm_tok = None
    for ts in range(NTS):
        par = ts % 2
        if upto <= 0:
            hfm_tok[ts] = stat_v[ts]
            continue
        P.emit("vector",
            lambda e, ts=ts: e.tensor_scalar(meanc.ap()[:, ts:ts + 1],
                                             s1c.ap()[:, ts:ts + 1], 1.0 / D, None, ALU.mult),
            waits=[stat_v[ts]])
        P.emit("vector",
            lambda e, ts=ts: e.tensor_scalar(ex2c.ap()[:, ts:ts + 1],
                                             s2c.ap()[:, ts:ts + 1], 1.0 / D, None, ALU.mult),
            waits=[stat_s[ts]])
        P.drain("vector")
        nv = P.emit("vector",
            lambda e, ts=ts: e.scalar_tensor_tensor(
                nvarc.ap()[:, ts:ts + 1], meanc.ap()[:, ts:ts + 1],
                meanc.ap()[:, ts:ts + 1], ex2c.ap()[:, ts:ts + 1],
                ALU.mult, ALU.subtract),
            inc=sm_v)
        P.emit("scalar",
            lambda e, ts=ts: e.activation(lnvc.ap()[:, ts:ts + 1], nvarc.ap()[:, ts:ts + 1],
                                          AF.Ln, bias=epscol.ap(), scale=-1.0),
            waits=[nv])
        P.drain("scalar")
        rs = P.emit("scalar",
            lambda e, ts=ts: e.activation(rstdc.ap()[:, ts:ts + 1], lnvc.ap()[:, ts:ts + 1],
                                          AF.Exp, scale=-0.5),
            inc=sm_s)
        if upto <= 1:
            hfm_tok[ts] = rs
            continue
        hw = [rs, hst_free[par], rep_tok]
        P.emit("vector",
            lambda e, ts=ts, par=par: e.tensor_scalar(
                h_stage[par].ap(), x_tm.ap()[:, ts * D:(ts + 1) * D],
                meanc.ap()[:, ts:ts + 1], rstdc.ap()[:, ts:ts + 1],
                ALU.subtract, ALU.mult),
            waits=hw)
        P.drain("vector")
        P.emit("vector",
            lambda e, par=par: e.tensor_tensor(h_stage[par].ap(), h_stage[par].ap(),
                                               scale_rep.ap(), ALU.mult))
        P.drain("vector")
        hs = P.emit("vector",
            lambda e, par=par: e.tensor_tensor(h_stage[par].ap(), h_stage[par].ap(),
                                               shift_rep.ap(), ALU.add),
            inc=sm_v)
        if upto <= 2:
            hfm_tok[ts] = hs
            continue
        last_ev = None
        for cb in range(2):
            ps = psA if cb == 0 else psB
            tw = [hs, ps_free[cb], ps_free_s[cb]]
            for i in range(4):
                c = cb * 4 + i
                P.emit("tensor",
                    lambda e, c=c, i=i, par=par, ps=ps: e.transpose(
                        ps.ap()[:, i * 128:(i + 1) * 128],
                        h_stage[par].ap()[:, c * 128:(c + 1) * 128],
                        id32.ap()),
                    waits=tw, inc=(sm_t if i == 3 else None))
                tw = []
            trk = sm_t.tok()
            if cb == 1:
                hst_free[par] = trk
            if upto <= 3:
                last_ev = trk
                ps_free[cb] = None
                hst_free[par] = None
                continue
            last_ev = P.emit("vector",
                lambda e, cb=cb, ts=ts, ps=ps: e.tensor_copy(
                    h_fm.ap().rearrange("p (c t j) -> p c t j", c=NCH, t=NTS)
                    [:, cb * 4:(cb + 1) * 4, ts:ts + 1, :],
                    ps.ap().rearrange("p (c j) -> p c j", c=4).unsqueeze(2)),
                waits=[trk], inc=sm_v)
            ps_free[cb] = last_ev
            if rt_f32 is not None:
                ps_free_s[cb] = P.emit("scalar",
                    lambda e, cb=cb, ps=ps: e.copy(
                        rt_f32.ap()[:, cb * 512:(cb + 1) * 512], ps.ap()),
                    waits=[trk, rtmm_tok, last_ev], inc=sm_s)
        hfm_tok[ts] = last_ev
        if rt_f32 is not None:
            for c in range(NCH):
                P.emit("tensor",
                    lambda e, c=c, ts=ts: e.matmul(
                        logits_ps.ap()[:, ts * 12:(ts + 1) * 12],
                        rt_f32.ap()[:, c * 128:(c + 1) * 128],
                        wr_sb.ap()[:, c * 12:(c + 1) * 12],
                        start=(ts == 0 and c == 0), stop=(ts == NTS - 1 and c == NCH - 1)),
                    waits=[ps_free_s[0], ps_free_s[1]],
                    inc=(sm_t if c == NCH - 1 else None))
            rtmm_tok = sm_t.tok()
    return rtmm_tok if rt_f32 is not None else hfm_tok[NTS - 1]


def build_program(stage=4.0, debug=False):
    nc = bass.Bass("TRN2", target_bir_lowering=False)
    P = Plan()

    di = {}
    def inp(name, shape, dt=F32):
        di[name] = nc.dram_tensor(name, list(shape), dt, kind="ExternalInput").ap()
        return di[name]

    x_d = inp("x", (T, D))
    inp("w_dq", (128, 1024)); inp("w_dqT", (1024, 128))
    inp("w_uq", (1024, 128)); inp("w_uqT", (128, 1024))
    inp("w_dkvT", (1024, 128)); inp("w_uk", (1024, 128))
    inp("w_uv", (1024, 128)); inp("w_oT", (1024, 1024))
    inp("wmod", (1024, 6144))
    inp("c_cols", (128, 8))
    inp("ws1", (1024, 4096))
    inp("ws2", (2048, 1024))
    inp("we1", (NE, 1024, 6144))
    inp("we2", (NE, 3072, 1024))
    inp("wr", (1024, 12))
    inp("br_rep", (128, 12))
    inp("bs1c", (128, 32))
    inp("be1c", (128, NE, 48))
    inp("be2p", (128, 1024))
    inp("bs2rp", (128, 1024))
    inp("ones1", (1, 128))
    inp("onescol", (128, 1))
    inp("epscol", (128, 1))
    inp("ident32", (128, 128))
    inp("identbf", (128, 128), BF16)
    inp("maskbf", (128, 128), BF16)
    out_d = nc.dram_tensor("out", [T, D], F32, kind="ExternalOutput").ap()

    with ExitStack() as top:
        def sballoc(ctx, name, shape, dt=F32):
            return ctx.enter_context(nc.sbuf_tensor(name, list(shape), dt))
        sb = lambda name, shape, dt=F32: sballoc(top, name, shape, dt)

        x_tm = sb("x_tm", (128, NTS * D))
        h_fm = sb("h_fm", (128, NCH * T), BF16)
        gb_rep = sb("gb_rep", (128, D), BF16)
        wr_sb = sb("wr_sb", (128, NCH * 12))
        br_sb = sb("br_sb", (128, 12))
        bs1c_sb = sb("bs1c_sb", (128, 32))
        id32 = sb("id32", (128, 128))
        idbf = sb("idbf", (128, 128), BF16)
        mask_sb = sb("mask_sb", (128, 128), BF16)
        ones1_sb = sb("ones1_sb", (1, 128))
        onescol_sb = sb("onescol_sb", (128, 1))
        epscol_sb = sb("epscol_sb", (128, 1))
        c_cols_sb = sb("c_cols_sb", (128, 8))
        silu_c = sb("silu_c", (128, 8))
        s1c = sb("s1c", (128, NTS)); s2c = sb("s2c", (128, NTS))
        meanc = sb("meanc", (128, NTS)); ex2c = sb("ex2c", (128, NTS))
        nvarc = sb("nvarc", (128, NTS)); lnvc = sb("lnvc", (128, NTS))
        rstdc = sb("rstdc", (128, NTS))
        sq_dump = sb("sq_dump", (128, D), BF16)
        rl = sb("rl", (128, 12)); rm1 = sb("rm1", (128, 1)); rm1n = sb("rm1n", (128, 1))
        req = sb("req", (128, 12)); rl2 = sb("rl2", (128, 12)); rm2 = sb("rm2", (128, 1))
        rex = sb("rex", (128, 12)); rmk = sb("rmk", (128, 12)); rp = sb("rp", (128, 12))
        rZs = sb("rZs", (128, 1)); rrZ = sb("rrZ", (128, 1))

        sm_t = Sem(nc, top, "sm_t")
        sm_v = Sem(nc, top, "sm_v")
        sm_s = Sem(nc, top, "sm_s")
        sm_px = Sem(nc, top, "sm_px")
        sm_st = Sem(nc, top, "sm_st")
        sm_ga = Sem(nc, top, "sm_ga")
        sm_w1 = [Sem(nc, top, "sm_w1a"), Sem(nc, top, "sm_w1b")]
        sm_w2 = [Sem(nc, top, "sm_w2a"), Sem(nc, top, "sm_w2b")]
        sm_wm = [Sem(nc, top, "sm_wma"), Sem(nc, top, "sm_wmb")]
        sm_gb = Sem(nc, top, "sm_gb")
        sm_gc = Sem(nc, top, "sm_gc")
        sm_wo = Sem(nc, top, "sm_wo")

        cols = (s1c, s2c, meanc, ex2c, nvarc, lnvc, rstdc)

        late = top.enter_context(ExitStack())
        w_rt = sballoc(late, "w_rt", (128, NTS * 12))
        wT = sballoc(late, "wTq", (12, T), BF16)
        be2s_bf = sballoc(late, "be2s", (128, D), BF16)
        be1c_sb = sballoc(late, "be1c_sb", (128, NE * 48), BF16)

        mid = ExitStack()
        k_eff = sballoc(mid, "k_eff", (128, 8 * 128), BF16)
        v_eff = sballoc(mid, "v_eff", (128, D), BF16)
        ca1_rep = sballoc(mid, "ca1_rep", (128, D)); sa_rep = sballoc(mid, "sa_rep", (128, D))
        cb1_rep = sballoc(mid, "cb1_rep", (128, D)); sb_rep = sballoc(mid, "sb_rep", (128, D))

        def ld(dst_ap, src_ap, waits=()):
            return P.emit("sync", lambda e, d=dst_ap, s=src_ap: e.dma_start(d, s),
                          waits=waits, inc=sm_px, inck=16)

        ld(id32.ap(), di["ident32"]); ld(idbf.ap(), di["identbf"])
        ld(mask_sb.ap(), di["maskbf"]); ld(ones1_sb.ap(), di["ones1"])
        ld(onescol_sb.ap(), di["onescol"]); ld(c_cols_sb.ap(), di["c_cols"])
        ld(epscol_sb.ap(), di["epscol"])
        ld(br_sb.ap(), di["br_rep"]); ld(bs1c_sb.ap(), di["bs1c"])
        for kc in range(NCH):
            ld(wr_sb.ap()[:, kc * 12:(kc + 1) * 12], di["wr"][kc * 128:(kc + 1) * 128, :])
        for ts in range(NTS):
            ld(x_tm.ap()[:, ts * D:(ts + 1) * D], x_d[ts * 128:(ts + 1) * 128, :])
        px_all = sm_px.tok()
        consts_tok = px_all
        x_tok = [px_all] * NTS

        def gld(dst_ap, src_ap, sem, waits=()):
            return P.emit("gpsimd", lambda e, d=dst_ap, s=src_ap: e.dma_start(d, s),
                          waits=waits, inc=sem, inck=16)

        def chunked(dst_sb, src_ap, nk):
            return gld(dst_sb.ap().rearrange("p (k c) -> p k c", k=nk),
                       src_ap.rearrange("(k p) c -> p k c", p=128), sm_ga)

        be1c_tok = gld(be1c_sb.ap().rearrange("p (e s) -> p e s", e=NE), di["be1c"], sm_gb)

        hfm_tok = [None] * NTS
        with ExitStack() as pre:
            psb = lambda name, shape, dt=F32: sballoc(pre, name, shape, dt)
            w_dq_sb = psb("w_dq_sb", (128, 1024), BF16)
            w_uq_sb = psb("w_uq_sb", (128, 8 * 128), BF16)
            w_uk_sb = psb("w_uk_sb", (128, 8 * 128), BF16)
            w_uv_sb = psb("w_uv_sb", (128, 8 * 128), BF16)
            w_oT_sb = psb("w_oT_sb", (128, 8 * 512), BF16)
            mod_row = psb("mod_row", (1, 6144))
            mrow_g1 = psb("mrow_g1", (1, 1024)); mrow_g2 = psb("mrow_g2", (1, 1024))
            wmodbuf = [psb(f"wmodb{i}", (128, 8 * 256), F32) for i in range(2)]
            ga_rep = psb("ga_rep", (128, D), BF16)
            h_stage = [psb(f"h_stage{i}", (128, D)) for i in range(2)]
            B1_bf = psb("B1_bf", (128, 128), BF16)

            gld(w_dq_sb.ap(), di["w_dq"], sm_ga)
            chunked(w_uq_sb, di["w_uq"], 8)
            chunked(w_uk_sb, di["w_uk"], 8)
            chunked(w_uv_sb, di["w_uv"], 8)
            attw_tok = sm_ga.tok()
            woT_tok0 = gld(w_oT_sb.ap().rearrange("p (k c) -> p k c", k=8),
                           di["w_oT"][:, 0:512].rearrange("(k p) c -> p k c", p=128),
                           sm_wo)

            silu_tok = P.emit("scalar",
                              lambda e: e.activation(silu_c.ap(), c_cols_sb.ap(), AF.Silu),
                              waits=[consts_tok], inc=sm_s)

            stat1_v = []; stat1_s = []
            for ts in range(NTS):
                xs = x_tm.ap()[:, ts * D:(ts + 1) * D]
                stat1_v.append(P.emit("vector",
                    lambda e, xs=xs, ts=ts: e.reduce_sum(s1c.ap()[:, ts:ts + 1], xs, axis=AX.X),
                    waits=[x_tok[ts]], inc=sm_v))
                stat1_s.append(P.emit("scalar",
                    lambda e, xs=xs, ts=ts: e.activation(sq_dump.ap(), xs, AF.Square,
                                                         accum_out=s2c.ap()[:, ts:ts + 1]),
                    waits=[x_tok[ts]], inc=sm_s))

            with nc.psum_tensor("mod_ps", [1, 256], F32) as mod_ps, \
                 nc.psum_tensor("pp1", [128, 512], F32) as pp1, \
                 nc.psum_tensor("pp2", [128, 512], F32) as pp2:

                NMB = 24 if stage >= 0.5 else 0
                mod_evac = [None] * max(NMB, 1)
                wm_cnt = [0, 0]
                wm_tok = {}

                def wmod_dma(j, waits):
                    sl = j % 2
                    wm_cnt[sl] += 1
                    src = di["wmod"][:, j * 256:(j + 1) * 256].rearrange("(k p) c -> p k c", p=128)
                    gld(wmodbuf[sl].ap().rearrange("p (k c) -> p k c", k=8), src,
                        sm_wm[sl], waits=waits)
                    wm_tok[j] = (sm_wm[sl], 16 * wm_cnt[sl])

                wmod_dma(0, [silu_tok])
                wmod_dma(1, [])
                for j in range(NMB):
                    sl = j % 2
                    for kc in range(NCH):
                        P.emit("tensor",
                            lambda e, kc=kc, sl=sl: e.matmul(
                                mod_ps.ap(),
                                silu_c.ap()[:, kc:kc + 1],
                                wmodbuf[sl].ap()[:, kc * 256:(kc + 1) * 256],
                                start=(kc == 0), stop=(kc == NCH - 1)),
                            waits=[wm_tok[j], silu_tok, mod_evac[j - 1] if j >= 1 else None],
                            inc=(sm_t if kc == NCH - 1 else None))
                    mm_tok = sm_t.tok()
                    mod_evac[j] = P.emit("vector",
                        lambda e, j=j: e.tensor_copy(mod_row.ap()[:, j * 256:(j + 1) * 256],
                                                     mod_ps.ap()),
                        waits=[mm_tok], inc=sm_v)
                    if j + 2 < NMB:
                        wmod_dma(j + 2, [mm_tok])
                mod_tok = mod_evac[NMB - 1] if NMB else None

                def vop(fn, waits=(), dr=False):
                    t = P.emit("vector", fn, waits=waits, inc=sm_v)
                    if dr:
                        P.drain("vector")
                    return t
                if stage < 0.6:
                    modp_tok = None
                    reps_tok = None
                    b1_tok = None
                    keff_tok = None
                    veff_tok = None
                if stage >= 0.6: vop(lambda e: e.tensor_scalar(mod_row.ap()[:, 1024:2048],
                                              mod_row.ap()[:, 1024:2048], 1.0, None, ALU.add),
                    waits=[mod_tok])
                vop(lambda e: e.tensor_scalar(mod_row.ap()[:, 4096:5120],
                                              mod_row.ap()[:, 4096:5120], 1.0, None, ALU.add),
                    dr=True)
                for off in ((2048, 5120) if stage >= 0.6 else ()):
                    g = mod_row.ap()[:, off:off + 1024]
                    vop(lambda e, g=g: e.tensor_scalar(mrow_g1.ap(), g, -1.0, None, ALU.mult), dr=True)
                    vop(lambda e, g=g: e.tensor_tensor(mrow_g1.ap(), g, mrow_g1.ap(), ALU.max), dr=True)
                    vop(lambda e: e.tensor_scalar(mrow_g1.ap(), mrow_g1.ap(), 0.25, None, ALU.max))
                    vop(lambda e, g=g: e.tensor_scalar(mrow_g2.ap(), g, 0.0, None, ALU.is_ge), dr=True)
                    vop(lambda e: e.tensor_scalar(mrow_g2.ap(), mrow_g2.ap(), 2.0, -1.0,
                                                  ALU.mult, ALU.add), dr=True)
                    vop(lambda e, g=g: e.tensor_tensor(g, mrow_g1.ap(), mrow_g2.ap(), ALU.mult),
                        dr=True)
                if stage >= 0.6:
                    modp_tok = sm_v.tok()

                reps = ([(sa_rep, 0), (ca1_rep, 1024), (ga_rep, 2048),
                        (sb_rep, 3072), (cb1_rep, 4096), (gb_rep, 5120)]
                        if stage >= 0.6 else [])
                prev_rep_ev = None
                for dst, off in reps:
                    for n2 in range(2):
                        mm = P.emit("tensor",
                            lambda e, off=off, n2=n2: e.matmul(
                                pp1.ap(), ones1_sb.ap(),
                                mod_row.ap()[:, off + n2 * 512: off + (n2 + 1) * 512],
                                start=True, stop=True),
                            waits=[modp_tok, consts_tok, prev_rep_ev], inc=sm_t)
                        prev_rep_ev = P.emit("vector",
                            lambda e, dst=dst, n2=n2: e.tensor_copy(
                                dst.ap()[:, n2 * 512:(n2 + 1) * 512], pp1.ap()),
                            waits=[mm], inc=sm_v)
                if stage >= 0.6:
                    reps_tok = prev_rep_ev

                for kc in (range(NCH) if stage >= 0.8 else ()):
                    P.emit("tensor",
                        lambda e, kc=kc: e.matmul(
                            pp2.ap()[:, 0:128],
                            w_uq_sb.ap()[:, kc * 128:(kc + 1) * 128],
                            w_uk_sb.ap()[:, kc * 128:(kc + 1) * 128],
                            start=(kc == 0), stop=(kc == NCH - 1)),
                        waits=[attw_tok], inc=(sm_t if kc == NCH - 1 else None))
                if stage >= 0.8:
                    b1_mm = sm_t.tok()
                    b1_tok = P.emit("vector",
                        lambda e: e.tensor_copy(B1_bf.ap(), pp2.ap()[:, 0:128]),
                        waits=[b1_mm], inc=sm_v)
                keff_ev = None
                for s in (range(NCH) if stage >= 0.8 else ()):
                    mm = P.emit("tensor",
                        lambda e, s=s: e.matmul(
                            pp2.ap()[:, 128:256],
                            w_dq_sb.ap()[:, s * 128:(s + 1) * 128],
                            B1_bf.ap(), start=True, stop=True),
                        waits=[b1_tok, keff_ev], inc=sm_t)
                    keff_ev = P.emit("vector",
                        lambda e, s=s: e.tensor_copy(k_eff.ap()[:, s * 128:(s + 1) * 128],
                                                     pp2.ap()[:, 128:256]),
                        waits=[mm], inc=sm_v)
                if stage >= 0.8:
                    keff_tok = keff_ev

                veff_ev = prev_rep_ev
                woT_tok = [woT_tok0, None]
                for n2 in (range(2) if stage >= 0.8 else ()):
                    for kc in range(NCH):
                        P.emit("tensor",
                            lambda e, kc=kc: e.matmul(
                                pp1.ap(),
                                w_uv_sb.ap()[:, kc * 128:(kc + 1) * 128],
                                w_oT_sb.ap()[:, kc * 512:(kc + 1) * 512],
                                start=(kc == 0), stop=(kc == NCH - 1)),
                            waits=[attw_tok, woT_tok[n2], reps_tok, veff_ev],
                            inc=(sm_t if kc == NCH - 1 else None))
                    mm = sm_t.tok()
                    if n2 == 0:
                        woT_tok[1] = gld(
                            w_oT_sb.ap().rearrange("p (k c) -> p k c", k=8),
                            di["w_oT"][:, 512:1024].rearrange("(k p) c -> p k c", p=128),
                            sm_wo, waits=[mm])
                    veff_ev = P.emit("vector",
                        lambda e, n2=n2: e.tensor_tensor(
                            v_eff.ap()[:, n2 * 512:(n2 + 1) * 512], pp1.ap(),
                            ga_rep.ap()[:, n2 * 512:(n2 + 1) * 512], ALU.mult),
                        waits=[mm], inc=sm_v)
                if stage >= 0.8:
                    veff_tok = veff_ev

            with nc.psum_tensor("lnpA", [128, 512], F32) as lnpA, \
                 nc.psum_tensor("lnpB", [128, 512], F32) as lnpB:
                if stage >= 0.9:
                    ln_chain(P, x_tm, cols, ca1_rep, sa_rep, h_stage, (lnpA, lnpB), id32,
                             h_fm, hfm_tok, stat1_v, stat1_s, reps_tok, sm_t, sm_v, sm_s,
                             epscol_sb)

        def partial_store(toks):
            for ts in range(NTS):
                P.emit("sync",
                    lambda e_, ts=ts: e_.dma_start(
                        out_d[ts * 128:(ts + 1) * 128, :],
                        x_tm.ap()[:, ts * D:(ts + 1) * D]),
                    waits=[toks[ts] if isinstance(toks, list) else toks,
                           hfm_tok[NTS - 1]] if hfm_tok[NTS - 1] else
                          [toks[ts] if isinstance(toks, list) else toks],
                    inc=sm_st, inck=16)
            P.emit("sync", lambda e: None, waits=[(sm_st, sm_st.n)])

        def finish():
            with nc.Block() as blk:
                blk.tensor(lambda e: [op(e) for op in P.ops["tensor"]])
                blk.vector(lambda e: [op(e) for op in P.ops["vector"]])
                blk.scalar(lambda e: [op(e) for op in P.ops["scalar"]])
                blk.sync(lambda e: [op(e) for op in P.ops["sync"]])
                blk.gpsimd(lambda e: [op(e) for op in P.ops["gpsimd"]])

        if stage <= 1.0:
            partial_store(x_tok)
            mid.close()
            finish()
            return nc

        xadd_tok = [None] * NTS
        with ExitStack() as att:
            asb = lambda name, shape, dt=BF16: sballoc(att, name, shape, dt)
            w_dqT_sb = asb("w_dqT_sb", (128, 8 * 128))
            w_uqT_sb = asb("w_uqT_sb", (128, 1024))
            w_dkvT_sb = asb("w_dkvT_sb", (128, 8 * 128))
            q_fm = asb("q_fm", (128, NCH * T))
            ckv_fm = asb("ckv_fm", (128, T))
            ckv_tm = asb("ckv_tm", (128, NTS * 128))
            gld(w_dqT_sb.ap().rearrange("p (k c) -> p k c", k=8),
                di["w_dqT"].rearrange("(k p) c -> p k c", p=128), sm_gc,
                waits=[veff_tok])
            gld(w_uqT_sb.ap(), di["w_uqT"], sm_gc)
            gld(w_dkvT_sb.ap().rearrange("p (k c) -> p k c", k=8),
                di["w_dkvT"].rearrange("(k p) c -> p k c", p=128), sm_gc)
            attw2_tok = sm_gc.tok()

            with ExitStack() as apre:
                q1_sb = sballoc(apre, "q1_sb", (128, T), BF16)
                with nc.psum_tensor("ap1", [128, 512], F32) as ap1, \
                     nc.psum_tensor("ap2", [128, 512], F32) as ap2:
                    ev1 = None
                    for tg in range(4):
                        for kc in range(NCH):
                            P.emit("tensor",
                                lambda e, kc=kc, tg=tg: e.matmul(
                                    ap1.ap(),
                                    w_dkvT_sb.ap()[:, kc * 128:(kc + 1) * 128],
                                    h_fm.ap()[:, kc * T + tg * 512: kc * T + (tg + 1) * 512],
                                    start=(kc == 0), stop=(kc == NCH - 1)),
                                waits=[attw2_tok, hfm_tok[4 * tg + 3], ev1],
                                inc=(sm_t if kc == NCH - 1 else None))
                        mm = sm_t.tok()
                        ev1 = P.emit("vector",
                            lambda e, tg=tg: e.tensor_copy(
                                ckv_fm.ap()[:, tg * 512:(tg + 1) * 512], ap1.ap()),
                            waits=[mm], inc=sm_v)
                    ev2 = None
                    for tg in range(4):
                        for kc in range(NCH):
                            P.emit("tensor",
                                lambda e, kc=kc, tg=tg: e.matmul(
                                    ap2.ap(),
                                    w_dqT_sb.ap()[:, kc * 128:(kc + 1) * 128],
                                    h_fm.ap()[:, kc * T + tg * 512: kc * T + (tg + 1) * 512],
                                    start=(kc == 0), stop=(kc == NCH - 1)),
                                waits=[ev2], inc=(sm_t if kc == NCH - 1 else None))
                        mm = sm_t.tok()
                        ev2 = P.emit("vector",
                            lambda e, tg=tg: e.tensor_copy(
                                q1_sb.ap()[:, tg * 512:(tg + 1) * 512], ap2.ap()),
                            waits=[mm], inc=sm_v)
                    q1_tok = ev2
                    ev3 = None
                    for ts in range(NTS):
                        for kc in range(NCH):
                            P.emit("tensor",
                                lambda e, kc=kc, ts=ts: e.matmul(
                                    ap1.ap()[:, 0:128],
                                    h_fm.ap()[:, kc * T + ts * 128: kc * T + (ts + 1) * 128],
                                    w_dkvT_sb.ap()[:, kc * 128:(kc + 1) * 128],
                                    start=(kc == 0), stop=(kc == NCH - 1)),
                                waits=[ev3], inc=(sm_t if kc == NCH - 1 else None))
                        mm = sm_t.tok()
                        ev3 = P.emit("vector",
                            lambda e, ts=ts: e.tensor_copy(
                                ckv_tm.ap()[:, ts * 128:(ts + 1) * 128], ap1.ap()[:, 0:128]),
                            waits=[mm], inc=sm_v)
                    ckvtm_tok = ev3
                    ev4 = None
                    for s in range(NCH):
                        for tg in range(4):
                            mm = P.emit("tensor",
                                lambda e, s=s, tg=tg: e.matmul(
                                    ap2.ap(),
                                    w_uqT_sb.ap()[:, s * 128:(s + 1) * 128],
                                    q1_sb.ap()[:, tg * 512:(tg + 1) * 512],
                                    start=True, stop=True),
                                waits=[q1_tok, ev4], inc=sm_t)
                            ev4 = P.emit("vector",
                                lambda e, s=s, tg=tg: e.tensor_copy(
                                    q_fm.ap()[:, s * T + tg * 512: s * T + (tg + 1) * 512],
                                    ap2.ap()),
                                waits=[mm], inc=sm_v)
                    qfm_tok = ev4

            ql_bf = asb("ql_bf", (128, 512))
            P_bf = [asb(f"P_bf{i}", (128, T)) for i in range(2)]
            PT_bf = asb("PT_bf", (128, 16 * 128))
            ctx_bf = [asb(f"ctx_bf{i}", (128, 128)) for i in range(2)]
            y_fm = asb("y_fm", (128, 8 * 512))
            m1col = sballoc(att, "m1col", (128, 4))
            lcol = sballoc(att, "lcol", (128, 4))
            rcol = sballoc(att, "rcol", (128, 4))

            with nc.psum_tensor("S_ps", [128, 2048], F32) as S_ps, \
                 nc.psum_tensor("ql_ps", [128, 512], F32) as ql_ps, \
                 nc.psum_tensor("PT_ps", [128, 1024], BF16) as PT_ps, \
                 nc.psum_tensor("ctx_ps", [128, 512], F32) as ctx_ps, \
                 nc.psum_tensor("y_ps", [128, 512], F32) as y_ps:

                exp_tok = None
                pt_done = [None, None]
                ql_evac = None
                ctx_evac = [None, None]
                ptps_free = None
                yfm_evac = None
                ymm_tok = None

                for g in range(NG):
                    for h in range(NH):
                        hp = h // 2
                        po = 64 * (h % 2)
                        mm = P.emit("tensor",
                            lambda e, hp=hp, po=po, g=g: e.matmul(
                                ql_ps.ap(),
                                k_eff.ap()[po:po + 64, hp * 128:(hp + 1) * 128],
                                q_fm.ap()[po:po + 64, hp * T + g * 512: hp * T + (g + 1) * 512],
                                start=True, stop=True),
                            waits=[qfm_tok, keff_tok, ql_evac], inc=sm_t)
                        ql_evac = P.emit("vector",
                            lambda e: e.tensor_copy(ql_bf.ap(), ql_ps.ap()),
                            waits=[mm], inc=sm_v)
                        for qtl in range(4):
                            qt = 4 * g + qtl
                            kmax = 128 * (qt + 1)
                            par = qtl % 2
                            nbank = (kmax + 511) // 512
                            s_waits = [ql_evac, exp_tok]
                            for b in range(nbank):
                                lo = b * 512
                                hi = min(kmax, lo + 512)
                                lastb = (hi == kmax)
                                P.emit("tensor",
                                    lambda e, lo=lo, hi=hi, qtl=qtl: e.matmul(
                                        S_ps.ap()[:, lo:hi],
                                        ql_bf.ap()[:, qtl * 128:(qtl + 1) * 128],
                                        ckv_fm.ap()[:, lo:hi],
                                        start=True, stop=(not lastb)),
                                    waits=s_waits, inc=None)
                                s_waits = []
                            smm = P.emit("tensor",
                                lambda e, qt=qt, kmax=kmax: e.matmul(
                                    S_ps.ap()[:, qt * 128:kmax],
                                    idbf.ap(), mask_sb.ap(),
                                    start=False, stop=True),
                                waits=[], inc=sm_t)
                            P.emit("vector",
                                lambda e, kmax=kmax, qtl=qtl: e.reduce_max(
                                    m1col.ap()[:, qtl:qtl + 1], S_ps.ap()[:, 0:kmax], axis=AX.X),
                                waits=[smm])
                            P.drain("vector")
                            mx = P.emit("vector",
                                lambda e, qtl=qtl: e.tensor_scalar(
                                    m1col.ap()[:, qtl:qtl + 1], m1col.ap()[:, qtl:qtl + 1],
                                    -0.125, None, ALU.mult),
                                inc=sm_v)
                            P.drain("vector")
                            exp_tok = P.emit("scalar",
                                lambda e, kmax=kmax, par=par, qtl=qtl: e.activation(
                                    P_bf[par].ap()[:, 0:kmax], S_ps.ap()[:, 0:kmax], AF.Exp,
                                    bias=m1col.ap()[:, qtl:qtl + 1], scale=0.125,
                                    accum_out=lcol.ap()[:, qtl:qtl + 1]),
                                waits=[mx, pt_done[par]], inc=sm_s)
                            P.emit("vector",
                                lambda e, qtl=qtl: e.reciprocal(
                                    rcol.ap()[:, qtl:qtl + 1], lcol.ap()[:, qtl:qtl + 1]),
                                waits=[exp_tok])
                            P.drain("vector")
                            pn = P.emit("vector",
                                lambda e, kmax=kmax, par=par, qtl=qtl: e.tensor_scalar(
                                    P_bf[par].ap()[:, 0:kmax], P_bf[par].ap()[:, 0:kmax],
                                    rcol.ap()[:, qtl:qtl + 1], None, ALU.mult),
                                inc=sm_v)
                            nsc = qt + 1
                            first_ctx = True
                            for scb in range(0, nsc, 4):
                                scs = list(range(scb, min(nsc, scb + 4)))
                                tw = [pn, ptps_free]
                                for i, sc in enumerate(scs):
                                    P.emit("tensor",
                                        lambda e, sc=sc, i=i, par=par: e.transpose(
                                            PT_ps.ap()[:, i * 128:(i + 1) * 128],
                                            P_bf[par].ap()[:, sc * 128:(sc + 1) * 128],
                                            idbf.ap()),
                                        waits=tw, inc=(sm_t if i == len(scs) - 1 else None))
                                    tw = []
                                trk = sm_t.tok()
                                ptb_evac = P.emit("scalar",
                                    lambda e, n=len(scs), scb=scb: e.copy(
                                        PT_bf.ap()[:, scb * 128:(scb + n) * 128],
                                        PT_ps.ap()[:, 0:n * 128]),
                                    waits=[trk], inc=sm_s)
                                ptps_free = ptb_evac
                                for sc in scs:
                                    cw = [ptb_evac]
                                    if first_ctx:
                                        cw += [ckvtm_tok, ctx_evac[0], ctx_evac[1]]
                                        first_ctx = False
                                    P.emit("tensor",
                                        lambda e, sc=sc, nsc=nsc: e.matmul(
                                            ctx_ps.ap()[:, 0:128],
                                            ckv_tm.ap()[:, sc * 128:(sc + 1) * 128],
                                            PT_bf.ap()[:, sc * 128:(sc + 1) * 128],
                                            start=(sc == 0), stop=(sc == nsc - 1)),
                                        waits=cw,
                                        inc=(sm_t if sc == nsc - 1 else None))
                            pt_done[par] = sm_t.tok()
                            ctx_mm = sm_t.tok()
                            ctx_evac[par] = P.emit("vector",
                                lambda e, par=par: e.tensor_copy(ctx_bf[par].ap(),
                                                                 ctx_ps.ap()[:, 0:128]),
                                waits=[ctx_mm], inc=sm_v)
                            ymm_tok = P.emit("tensor",
                                lambda e, po=po, qtl=qtl, h=h, par=par: e.matmul(
                                    y_ps.ap()[po:po + 64, qtl * 128:(qtl + 1) * 128],
                                    v_eff.ap()[:, h * 64:(h + 1) * 64],
                                    ctx_bf[par].ap(),
                                    start=True, stop=True),
                                waits=[ctx_evac[par], veff_tok,
                                       yfm_evac if (h % 2 == 0 and qtl == 0) else None],
                                inc=sm_t)
                        if h % 2 == 1:
                            yfm_evac = P.emit("vector",
                                lambda e, hp=hp: e.tensor_copy(
                                    y_fm.ap()[:, hp * 512:(hp + 1) * 512], y_ps.ap()),
                                waits=[ymm_tok], inc=sm_v)
                    for j in range(4):
                        ts = 4 * g + j
                        prev_add = None
                        for cb in range(2):
                            tw = [yfm_evac, ptps_free, prev_add]
                            for i in range(4):
                                c = cb * 4 + i
                                P.emit("tensor",
                                    lambda e, c=c, i=i, j=j: e.transpose(
                                        PT_ps.ap()[:, i * 128:(i + 1) * 128],
                                        y_fm.ap()[:, c * 512 + j * 128: c * 512 + (j + 1) * 128],
                                        idbf.ap()),
                                    waits=tw, inc=(sm_t if i == 3 else None))
                                tw = []
                            trk = sm_t.tok()
                            prev_add = P.emit("vector",
                                lambda e, ts=ts, cb=cb: e.tensor_tensor(
                                    x_tm.ap()[:, ts * D + cb * 512: ts * D + (cb + 1) * 512],
                                    x_tm.ap()[:, ts * D + cb * 512: ts * D + (cb + 1) * 512],
                                    PT_ps.ap()[:, 0:512], ALU.add),
                                waits=[trk], inc=sm_v)
                            ptps_free = prev_add
                        xadd_tok[ts] = prev_add

                if debug:
                    allw = [xadd_tok[NTS - 1], (sm_v, sm_v.n), (sm_s, sm_s.n),
                            (sm_t, sm_t.n)]
                    def dbg_store(name, sbuf, shape, dt):
                        dd = nc.dram_tensor("dbg_" + name, list(shape), dt,
                                            kind="ExternalOutput").ap()
                        P.emit("sync",
                               lambda e, dd=dd, a=sbuf.ap(): e.dma_start(dd, a),
                               waits=allw, inc=sm_st, inck=16)
                    dbg_store("h_fm", h_fm, (128, NCH * T), BF16)
                    dbg_store("ckv_fm", ckv_fm, (128, T), BF16)
                    dbg_store("ckv_tm", ckv_tm, (128, NTS * 128), BF16)
                    dbg_store("q_fm", q_fm, (128, NCH * T), BF16)
                    dbg_store("k_eff", k_eff, (128, 8 * 128), BF16)
                    dbg_store("v_eff", v_eff, (128, D), BF16)
                    dbg_store("P_bf", P_bf[1], (128, T), BF16)
                    dbg_store("PT_bf", PT_bf, (128, 16 * 128), BF16)
                    dbg_store("y_fm", y_fm, (128, 8 * 512), BF16)
                    dbg_store("ca1", ca1_rep, (128, D), F32)
                    dbg_store("sa", sa_rep, (128, D), F32)

        if stage <= 2:
            partial_store(xadd_tok)
            mid.close()
            finish()
            return nc

        hfm2_tok = [None] * NTS
        with ExitStack() as ln2s:
            rt_f32 = sballoc(ln2s, "rt_f32", (128, D))
            h_stage2 = [sballoc(ln2s, f"h2st{i}", (128, D)) for i in range(2)]
            be2_f = sballoc(ln2s, "be2_f", (128, 1024))
            bs2r_f = sballoc(ln2s, "bs2r_f", (128, 1024))
            if stage >= 2.508:
                t1 = ld(be2_f.ap(), di["be2p"], waits=[xadd_tok[NTS - 1]])
                t2 = ld(bs2r_f.ap(), di["bs2rp"])
                if stage >= 2.5085 or stage >= 2.52:
                    be2s_tok = P.emit("vector",
                        lambda e: e.tensor_tensor(be2s_bf.ap(), be2_f.ap(), bs2r_f.ap(), ALU.add),
                        waits=[t1, t2], inc=sm_v)
                else:
                    be2s_tok = t2
            else:
                be2s_tok = None
            with nc.psum_tensor("l2A", [128, 512], F32) as l2A, \
                 nc.psum_tensor("l2B", [128, 512], F32) as l2B, \
                 nc.psum_tensor("logits_ps", [128, NTS * 12], F32) as logits_ps:
                stat2_v = []; stat2_s = []
                for ts in range(NTS):
                    xs = x_tm.ap()[:, ts * D:(ts + 1) * D]
                    if stage >= 2.515:
                        stat2_v.append(P.emit("vector",
                            lambda e, xs=xs, ts=ts: e.reduce_sum(s1c.ap()[:, ts:ts + 1], xs, axis=AX.X),
                            waits=[xadd_tok[ts]], inc=sm_v))
                    else:
                        stat2_v.append(xadd_tok[ts])
                    if stage >= 2.518:
                        stat2_s.append(P.emit("scalar",
                            lambda e, xs=xs, ts=ts: e.activation(sq_dump.ap(), xs, AF.Square,
                                                                 accum_out=s2c.ap()[:, ts:ts + 1]),
                            waits=[xadd_tok[ts]], inc=sm_s))
                    else:
                        stat2_s.append(xadd_tok[ts])
                upto_map = {2.505: 0, 2.508: 0, 2.5085: 0, 2.51: 0, 2.515: 0, 2.518: 0, 2.52: 1, 2.55: 2, 2.58: 3}
                ln2_upto = upto_map.get(stage, 4)
                use_rt = stage >= 2.7
                logits_done = ln_chain(P, x_tm, cols, cb1_rep, sb_rep, h_stage2,
                                       (l2A, l2B), id32, h_fm, hfm2_tok,
                                       stat2_v, stat2_s, reps_tok, sm_t, sm_v, sm_s,
                                       epscol_sb,
                                       rt_f32=(rt_f32 if use_rt else None),
                                       wr_sb=wr_sb, logits_ps=logits_ps,
                                       upto=ln2_upto)

                wt_tok = None
                for ts in (range(NTS) if stage >= 2.9 else ()):
                    P.emit("vector",
                        lambda e, ts=ts: e.tensor_tensor(
                            rl.ap(), logits_ps.ap()[:, ts * 12:(ts + 1) * 12],
                            br_sb.ap(), ALU.add),
                        waits=[logits_done])
                    P.drain("vector")
                    P.emit("vector", lambda e: e.reduce_max(rm1.ap(), rl.ap(), axis=AX.X))
                    P.drain("vector")
                    P.emit("vector",
                           lambda e: e.tensor_scalar(rm1n.ap(), rm1.ap(), -1.0 / TAU, None, ALU.mult))
                    P.emit("vector",
                           lambda e: e.tensor_scalar(req.ap(), rl.ap(), rm1.ap(), None, ALU.is_equal))
                    P.drain("vector")
                    P.emit("vector",
                           lambda e: e.scalar_tensor_tensor(rl2.ap(), req.ap(), -1e30, rl.ap(),
                                                            ALU.mult, ALU.add))
                    P.drain("vector")
                    P.emit("vector", lambda e: e.reduce_max(rm2.ap(), rl2.ap(), axis=AX.X))
                    P.drain("vector")
                    mk = P.emit("vector",
                        lambda e: e.tensor_scalar(rmk.ap(), rl.ap(), rm2.ap(), None, ALU.is_ge),
                        inc=sm_v)
                    ex = P.emit("scalar",
                        lambda e: e.activation(rex.ap(), rl.ap(), AF.Exp,
                                               bias=rm1n.ap(), scale=1.0 / TAU),
                        waits=[mk], inc=sm_s)
                    P.emit("vector",
                           lambda e: e.tensor_tensor(rp.ap(), rex.ap(), rmk.ap(), ALU.mult),
                           waits=[ex])
                    P.drain("vector")
                    P.emit("vector", lambda e: e.reduce_sum(rZs.ap(), rp.ap(), axis=AX.X))
                    P.drain("vector")
                    P.emit("vector", lambda e: e.reciprocal(rrZ.ap(), rZs.ap()))
                    P.drain("vector")
                    wv = P.emit("vector",
                        lambda e, ts=ts: e.tensor_scalar(
                            w_rt.ap()[:, ts * 12:(ts + 1) * 12], rp.ap(), rrZ.ap(),
                            None, ALU.mult),
                        inc=sm_v)
                    P.drain("vector")
                    wtm = P.emit("tensor",
                        lambda e, ts=ts: e.transpose(
                            l2A.ap()[0:12, 0:128],
                            w_rt.ap()[:, ts * 12:(ts + 1) * 12], id32.ap()),
                        waits=[wv, wt_tok], inc=sm_t)
                    wt_tok = P.emit("vector",
                        lambda e, ts=ts: e.tensor_copy(
                            wT.ap()[:, ts * 128:(ts + 1) * 128], l2A.ap()[0:12, 0:128]),
                        waits=[wtm], inc=sm_v)
                routing_tok = wt_tok if stage >= 2.9 else logits_done

        if stage <= 3.59:
            partial_store(routing_tok)
            mid.close()
            finish()
            return nc

        mid.close()

        with ExitStack() as ffn:
            fsb = lambda name, shape, dt=BF16: sballoc(ffn, name, shape, dt)
            act = fsb("act", (128, 24 * TT))
            T_acc = fsb("T_acc", (128, NTSH * D))
            W1buf = [fsb(f"W1buf{i}", (128, 8 * 256)) for i in range(2)]
            W2buf = [fsb(f"W2buf{i}", (128, 24 * 128)) for i in range(2)]
            yE0 = fsb("yE_sb0", (128, 4 * TT))
            yE_sb = [yE0, yE0]
            gg_sb = [fsb(f"gg{i}", (128, TT)) for i in range(2)]

            def npairs(e):
                return 16 if e == 0 else 24

            def nkc2(e):
                return 16 if e == 0 else 24

            hfm_all = hfm2_tok[NTS - 1]
            final_tok_prev = [None] * NTSH

            with nc.psum_tensor("Pg", [128, TT], F32) as Pg, \
                 nc.psum_tensor("Pa", [128, TT], F32) as Pa, \
                 nc.psum_tensor("yE_ps", [128, TT], F32) as yE_ps, \
                 nc.psum_tensor("tr_psA", [128, 1024], BF16) as tr_psA, \
                 nc.psum_tensor("tr_psB", [128, 1024], BF16) as tr_psB:
                tr_ps2 = [tr_psA, tr_psB]

                w1_cnt = [0, 0]
                w2_cnt = [0, 0]
                w1_slot_use = [None, None]
                w2_slot_use = [None, None]
                gelu_tok = None
                prev_stt = None
                yE_ev_v = None
                yE_ev_s = None
                yslot_free = [None, None]
                trhalf_tok = [None, None]

                for th in range(2):
                    toff = th * TT
                    tinit = [None] * NTSH
                    prev_cp = None
                    for ts in range(NTSH):
                        gts = th * NTSH + ts
                        for n2 in range(2):
                            P.emit("tensor",
                                lambda e, gts=gts, n2=n2: e.matmul(
                                    yE_ps.ap()[:, n2 * 512:(n2 + 1) * 512],
                                    wT.ap()[:, gts * 128:(gts + 1) * 128],
                                    be2s_bf.ap()[0:12, n2 * 512:(n2 + 1) * 512],
                                    start=True, stop=True),
                                waits=[routing_tok, be2s_tok, prev_cp, yE_ev_v, yE_ev_s],
                                inc=(sm_t if n2 == 1 else None))
                        mm = sm_t.tok()
                        prev_cp = P.emit("scalar",
                            lambda e, ts=ts: e.copy(
                                T_acc.ap()[:, ts * D:(ts + 1) * D], yE_ps.ap()),
                            waits=[mm, final_tok_prev[ts]], inc=sm_s)
                        tinit[ts] = prev_cp
                    yep_free = prev_cp

                    w1_plan = [(e, pr) for e in range(13) for pr in range(npairs(e))]
                    w2_plan = [(e, fs) for e in range(13) for fs in range(8)]
                    w1_tok = {}
                    w2_tok = {}
                    w1_i = [0]
                    w2_i = [0]

                    def w1_issue():
                        if w1_i[0] >= len(w1_plan):
                            return
                        e, pr = w1_plan[w1_i[0]]
                        sl = w1_i[0] % 2
                        w1_cnt[sl] += 1
                        src = (di["ws1"][:, pr * 256:(pr + 1) * 256] if e == 0
                               else di["we1"][e - 1, :, pr * 256:(pr + 1) * 256])
                        gld(W1buf[sl].ap().rearrange("p (k c) -> p k c", k=8),
                            src.rearrange("(k p) c -> p k c", p=128),
                            sm_w1[sl], waits=[w1_slot_use[sl], routing_tok])
                        w1_tok[(e, pr)] = ((sm_w1[sl], 16 * w1_cnt[sl]), sl)
                        w1_i[0] += 1

                    def w2_issue():
                        if w2_i[0] >= len(w2_plan):
                            return
                        e, fs = w2_plan[w2_i[0]]
                        sl = w2_i[0] % 2
                        kk = nkc2(e)
                        src = (di["ws2"][:, fs * 128:(fs + 1) * 128] if e == 0
                               else di["we2"][e - 1, :, fs * 128:(fs + 1) * 128])
                        src3 = src.rearrange("(k p) c -> p k c", p=128)
                        gate = [w2_slot_use[sl], routing_tok]
                        for k0 in range(0, kk, 8):
                            w2_cnt[sl] += 1
                            gld(W2buf[sl].ap()[:, k0 * 128:(k0 + 8) * 128]
                                .rearrange("p (k c) -> p k c", k=8),
                                src3[:, k0:k0 + 8, :],
                                sm_w2[sl], waits=gate)
                            gate = []
                        w2_tok[(e, fs)] = ((sm_w2[sl], 16 * w2_cnt[sl]), sl)
                        w2_i[0] += 1

                    w1_issue(); w1_issue()
                    w2_issue(); w2_issue()

                    for e in range(13 if stage >= 3.4 else 0):
                        np_ = npairs(e)
                        kk = nkc2(e)
                        stt_last = None
                        for pr in range(np_):
                            dtok, sl = w1_tok[(e, pr)]
                            par = pr % 2
                            first_w = [dtok, hfm_all, gelu_tok]
                            for kc in range(NCH):
                                for tt2 in range(2):
                                    P.emit("tensor",
                                        lambda e_, sl=sl, kc=kc, tt2=tt2, toff=toff: e_.matmul(
                                            Pg.ap()[:, tt2 * 512:(tt2 + 1) * 512],
                                            W1buf[sl].ap()[:, kc * 256 + 128: kc * 256 + 256],
                                            h_fm.ap()[:, kc * T + toff + tt2 * 512:
                                                      kc * T + toff + (tt2 + 1) * 512],
                                            start=(kc == 0), stop=(kc == NCH - 1)),
                                        waits=first_w,
                                        inc=(sm_t if (kc == NCH - 1 and tt2 == 1) else None))
                                    first_w = []
                            gmm = sm_t.tok()
                            bcol = (bs1c_sb.ap()[:, 2 * pr + 1: 2 * pr + 2] if e == 0
                                    else be1c_sb.ap()[:, (e - 1) * 48 + 2 * pr + 1:
                                                      (e - 1) * 48 + 2 * pr + 2])
                            gelu_tok = P.emit("scalar",
                                lambda e_, par=par, bcol=bcol: e_.activation(
                                    gg_sb[par].ap(), Pg.ap(), AF.Gelu, bias=bcol),
                                waits=[gmm, be1c_tok], inc=sm_s)
                            first_w = [prev_stt]
                            for kc in range(NCH):
                                for tt2 in range(2):
                                    P.emit("tensor",
                                        lambda e_, sl=sl, kc=kc, tt2=tt2, toff=toff: e_.matmul(
                                            Pa.ap()[:, tt2 * 512:(tt2 + 1) * 512],
                                            W1buf[sl].ap()[:, kc * 256: kc * 256 + 128],
                                            h_fm.ap()[:, kc * T + toff + tt2 * 512:
                                                      kc * T + toff + (tt2 + 1) * 512],
                                            start=(kc == 0), stop=(kc == NCH - 1)),
                                        waits=first_w,
                                        inc=(sm_t if (kc == NCH - 1 and tt2 == 1) else None))
                                    first_w = []
                            amm = sm_t.tok()
                            w1_slot_use[sl] = amm
                            acol = (bs1c_sb.ap()[:, 2 * pr: 2 * pr + 1] if e == 0
                                    else be1c_sb.ap()[:, (e - 1) * 48 + 2 * pr:
                                                      (e - 1) * 48 + 2 * pr + 1])
                            prev_stt = P.emit("vector",
                                lambda e_, pr=pr, par=par, acol=acol: e_.scalar_tensor_tensor(
                                    act.ap()[:, pr * TT:(pr + 1) * TT],
                                    Pa.ap(), acol, gg_sb[par].ap(),
                                    ALU.add, ALU.mult),
                                waits=[gelu_tok, amm], inc=sm_v)
                            stt_last = prev_stt
                            w1_issue()
                        for fs in range(8 if stage >= 3.6 else 0):
                            dtok2, sl2 = w2_tok[(e, fs)]
                            yslot = (fs // 4) % 2
                            first_w = [dtok2, yE_ev_v, yE_ev_s, stt_last, yep_free]
                            if fs % 4 == 0:
                                first_w.append(yslot_free[yslot])
                            for kc in range(kk):
                                for tt2 in range(2):
                                    P.emit("tensor",
                                        lambda e_, sl2=sl2, kc=kc, tt2=tt2: e_.matmul(
                                            yE_ps.ap()[:, tt2 * 512:(tt2 + 1) * 512],
                                            W2buf[sl2].ap()[:, kc * 128:(kc + 1) * 128],
                                            act.ap()[:, kc * TT + tt2 * 512:
                                                     kc * TT + (tt2 + 1) * 512],
                                            start=(kc == 0), stop=(kc == kk - 1)),
                                        waits=first_w,
                                        inc=(sm_t if (kc == kk - 1 and tt2 == 1) else None))
                                    first_w = []
                            ymm2 = sm_t.tok()
                            w2_slot_use[sl2] = ymm2
                            yE_ev_v = P.emit("vector",
                                lambda e_, yslot=yslot, fs=fs: e_.tensor_copy(
                                    yE_sb[yslot].ap()[:, (fs % 4) * TT: (fs % 4) * TT + 512],
                                    yE_ps.ap()[:, 0:512]),
                                waits=[ymm2], inc=sm_v)
                            yE_ev_s = P.emit("scalar",
                                lambda e_, yslot=yslot, fs=fs: e_.copy(
                                    yE_sb[yslot].ap()[:, (fs % 4) * TT + 512: (fs % 4 + 1) * TT],
                                    yE_ps.ap()[:, 512:1024]),
                                waits=[ymm2], inc=sm_s)
                            w2_issue()
                            if fs % 4 == 3 and stage >= 3.8:
                                fsb_ = fs // 4
                                last_stt = None
                                for ts in range(NTSH):
                                    gts = th * NTSH + ts
                                    half = ts % 2
                                    tw = [yE_ev_v, yE_ev_s, trhalf_tok[half]]
                                    for i in range(4):
                                        P.emit("tensor",
                                            lambda e_, yslot=yslot, i=i, ts=ts, half=half: e_.transpose(
                                                tr_ps2[half].ap()[:, i * 128:(i + 1) * 128],
                                                yE_sb[yslot].ap()[:, i * TT + ts * 128:
                                                                  i * TT + (ts + 1) * 128],
                                                idbf.ap()),
                                            waits=tw, inc=(sm_t if i == 3 else None))
                                        tw = []
                                    trk = sm_t.tok()
                                    if stage < 3.82:
                                        last_stt = P.emit("vector",
                                            lambda e_, half=half: e_.tensor_copy(
                                                gg_sb[0].ap()[:, 0:512],
                                                tr_ps2[half].ap()[:, 0:512]),
                                            waits=[trk], inc=sm_v)
                                        trhalf_tok[half] = last_stt
                                        continue
                                    wcol = (onescol_sb.ap() if e == 0
                                            else w_rt.ap()[:, gts * 12 + (e - 1): gts * 12 + e])
                                    last_stt = P.emit("vector",
                                        lambda e_, ts=ts, fsb_=fsb_, half=half, wcol=wcol:
                                        e_.scalar_tensor_tensor(
                                            T_acc.ap()[:, ts * D + fsb_ * 512:
                                                       ts * D + (fsb_ + 1) * 512],
                                            tr_ps2[half].ap()[:, 0:512],
                                            wcol,
                                            T_acc.ap()[:, ts * D + fsb_ * 512:
                                                       ts * D + (fsb_ + 1) * 512],
                                            ALU.mult, ALU.add),
                                        waits=[trk, tinit[ts]], inc=sm_v)
                                    trhalf_tok[half] = last_stt
                                yslot_free[0] = last_stt
                                yslot_free[1] = last_stt

                    for ts in range(NTSH if stage >= 3.9 else 0):
                        gts = th * NTSH + ts
                        P.emit("vector",
                            lambda e_, ts=ts: e_.tensor_tensor(
                                gg_sb[0].ap(), T_acc.ap()[:, ts * D:(ts + 1) * D],
                                gb_rep.ap(), ALU.mult))
                        P.drain("vector")
                        ftok = P.emit("vector",
                            lambda e_, gts=gts: e_.tensor_tensor(
                                x_tm.ap()[:, gts * D:(gts + 1) * D],
                                x_tm.ap()[:, gts * D:(gts + 1) * D],
                                gg_sb[0].ap(), ALU.add),
                            inc=sm_v)
                        final_tok_prev[ts] = ftok
                        P.emit("sync",
                            lambda e_, gts=gts: e_.dma_start(
                                out_d[gts * 128:(gts + 1) * 128, :],
                                x_tm.ap()[:, gts * D:(gts + 1) * D]),
                            waits=[ftok], inc=sm_st, inck=16)
                    gelu_tok = final_tok_prev[NTSH - 1]

        if stage >= 3.9:
            P.emit("sync", lambda e: None, waits=[(sm_st, 16 * NTS)])
        else:
            partial_store((sm_v, sm_v.n))
            for eng, sem in (("sync", sm_t), ("sync", sm_s)):
                P.emit(eng, lambda e: None, waits=[(sem, sem.n)])
        finish()
    return nc


_CACHE = {}


def _pair_permute_cols(w):
    shp = w.shape
    n2 = shp[-1]
    npr = n2 // 256
    w2 = w.reshape(shp[:-1] + (2, npr, 128))
    w3 = np.moveaxis(w2, -3, -2)
    return np.ascontiguousarray(w3.reshape(shp))


def kernel(L0_attn_x, L0_attn_W_dq, L0_attn_W_uq, L0_attn_W_dkv, L0_attn_W_uk,
           L0_attn_W_uv, L0_attn_W_o, L0_ffn_Ws1, L0_ffn_bs1, L0_ffn_Ws2,
           L0_ffn_bs2, L0_ffn_We1, L0_ffn_be1, L0_ffn_We2, L0_ffn_be2,
           L0_ffn_Wr, L0_ffn_br, L0_c, L0_wmod):
    f = np.float32
    asc = np.ascontiguousarray
    x = np.asarray(L0_attn_x, dtype=f)

    ws1p = _pair_permute_cols(np.asarray(L0_ffn_Ws1, f))
    we1p = _pair_permute_cols(np.asarray(L0_ffn_We1, f))
    bs1p = _pair_permute_cols(np.asarray(L0_ffn_bs1, f).reshape(1, -1))[0]
    be1p = _pair_permute_cols(np.asarray(L0_ffn_be1, f))

    mask = np.zeros((128, 128), f)
    mask[np.triu_indices(128, 1)] = -1e30

    common = {
        "w_dq": asc(np.asarray(L0_attn_W_dq, f)),
        "w_dqT": asc(np.asarray(L0_attn_W_dq, f).T),
        "w_uq": asc(np.asarray(L0_attn_W_uq, f)),
        "w_uqT": asc(np.asarray(L0_attn_W_uq, f).T),
        "w_dkvT": asc(np.asarray(L0_attn_W_dkv, f).T),
        "w_uk": asc(np.asarray(L0_attn_W_uk, f)),
        "w_uv": asc(np.asarray(L0_attn_W_uv, f)),
        "w_oT": asc(np.asarray(L0_attn_W_o, f).T),
        "wmod": asc(np.asarray(L0_wmod, f)),
        "ws1": ws1p,
        "ws2": asc(np.asarray(L0_ffn_Ws2, f)),
        "we1": we1p,
        "we2": asc(np.asarray(L0_ffn_We2, f)),
        "wr": asc(np.asarray(L0_ffn_Wr, f)),
        "br_rep": asc(np.broadcast_to(np.asarray(L0_ffn_br, f), (128, 12))),
        "bs1c": asc(bs1p.reshape(32, 128).T),
        "be1c": asc(be1p.reshape(NE, 48, 128).transpose(2, 0, 1)),
        "be2p": np.concatenate([np.asarray(L0_ffn_be2, f),
                                np.zeros((128 - NE, 1024), f)], axis=0),
        "bs2rp": asc(np.broadcast_to(np.asarray(L0_ffn_bs2, f), (128, 1024))),
        "ones1": np.ones((1, 128), f),
        "onescol": np.ones((128, 1), f),
        "ident32": np.eye(128, dtype=f),
        "epscol": np.full((128, 1), 1e-5, dtype=f),
        "identbf": np.eye(128, dtype=ml_dtypes.bfloat16),
        "maskbf": mask.astype(ml_dtypes.bfloat16),
    }
    c = np.asarray(L0_c, f)
    in_maps = []
    for b in range(8):
        m = dict(common)
        m["x"] = asc(x[b])
        m["c_cols"] = asc(c[b].reshape(8, 128).T)
        in_maps.append(m)

    import os as _os
    _stage = float(_os.environ.get("MK_STAGE", "4"))
    if _CACHE.get("stage") != _stage:
        _CACHE["nc"] = build_program(_stage)
        _CACHE["stage"] = _stage
    nc = _CACHE["nc"]
    res = run_bass_kernel_spmd(nc, in_maps, core_ids=list(range(8)))
    return np.stack([res.results[b]["out"] for b in range(8)], axis=0)


# version 02
# speedup vs baseline: 1.0627x; 1.0627x over previous
import numpy as np
import ml_dtypes
from contextlib import ExitStack

import concourse.bass as bass
import concourse.mybir as mybir
from concourse.bass_utils import run_bass_kernel_spmd

F32 = mybir.dt.float32
BF16 = mybir.dt.bfloat16
AF = mybir.ActivationFunctionType
ALU = mybir.AluOpType
AX = mybir.AxisListType

T = 2048
D = 1024
NTS = 16
NCH = 8
NH = 16
NE = 12
TAU = 1.5
TT = 1024
NTSH = 8
NG = 4
EPS = 1e-5


class Sem:
    def __init__(self, nc, ctx, name):
        self.h = ctx.enter_context(nc.semaphore(name))
        self.name = name
        self.n = 0

    def tok(self):
        return (self, self.n)


class Plan:
    ENGS = ("tensor", "vector", "scalar", "sync", "gpsimd")

    def __init__(self):
        self.ops = {e: [] for e in self.ENGS}
        self.seen = {e: {} for e in self.ENGS}

    def emit(self, eng, fn, waits=(), inc=None, inck=1):
        wl = []
        seen = self.seen[eng]
        for w in waits:
            if w is None:
                continue
            s, v = w
            if v <= 0 or seen.get(s.name, 0) >= v:
                continue
            assert v <= s.n, f"wait on future token {s.name}:{v}>{s.n}"
            seen[s.name] = v
            wl.append((s.h, v))
        if inc is not None:
            inc.n += inck

        def op(e, wl=wl, fn=fn, inc=inc, inck=inck):
            for h, v in wl:
                e.wait_ge(h, v)
            inst = fn(e)
            if inc is not None:
                inst.then_inc(inc.h, inck)
        self.ops[eng].append(op)
        return inc.tok() if inc is not None else None

    def drain(self, eng):
        self.ops[eng].append(lambda e: e.drain())


def ln_chain(P, x_tm, cols, scale_rep, shift_rep, h_stage, psums, id32,
             h_fm, hfm_tok, stat_v, stat_s, rep_tok, sm_t, sm_v, sm_s, epscol,
             rt_f32=None, wr_sb=None, logits_ps=None, upto=4):
    s1c, s2c, meanc, ex2c, nvarc, lnvc, rstdc = cols
    psA, psB = psums
    ps_free = [None, None]
    ps_free_s = [None, None]
    hst_free = [None, None]
    rtmm_tok = None
    for ts in range(NTS):
        par = ts % 2
        if upto <= 0:
            hfm_tok[ts] = stat_v[ts]
            continue
        P.emit("vector",
            lambda e, ts=ts: e.tensor_scalar(meanc.ap()[:, ts:ts + 1],
                                             s1c.ap()[:, ts:ts + 1], 1.0 / D, None, ALU.mult),
            waits=[stat_v[ts]])
        P.emit("vector",
            lambda e, ts=ts: e.tensor_scalar(ex2c.ap()[:, ts:ts + 1],
                                             s2c.ap()[:, ts:ts + 1], 1.0 / D, None, ALU.mult),
            waits=[stat_s[ts]])
        P.drain("vector")
        nv = P.emit("vector",
            lambda e, ts=ts: e.scalar_tensor_tensor(
                nvarc.ap()[:, ts:ts + 1], meanc.ap()[:, ts:ts + 1],
                meanc.ap()[:, ts:ts + 1], ex2c.ap()[:, ts:ts + 1],
                ALU.mult, ALU.subtract),
            inc=sm_v)
        P.emit("scalar",
            lambda e, ts=ts: e.activation(lnvc.ap()[:, ts:ts + 1], nvarc.ap()[:, ts:ts + 1],
                                          AF.Ln, bias=epscol.ap(), scale=-1.0),
            waits=[nv])
        P.drain("scalar")
        rs = P.emit("scalar",
            lambda e, ts=ts: e.activation(rstdc.ap()[:, ts:ts + 1], lnvc.ap()[:, ts:ts + 1],
                                          AF.Exp, scale=-0.5),
            inc=sm_s)
        if upto <= 1:
            hfm_tok[ts] = rs
            continue
        hw = [rs, hst_free[par], rep_tok]
        P.emit("vector",
            lambda e, ts=ts, par=par: e.tensor_scalar(
                h_stage[par].ap(), x_tm.ap()[:, ts * D:(ts + 1) * D],
                meanc.ap()[:, ts:ts + 1], rstdc.ap()[:, ts:ts + 1],
                ALU.subtract, ALU.mult),
            waits=hw)
        P.drain("vector")
        P.emit("vector",
            lambda e, par=par: e.tensor_tensor(h_stage[par].ap(), h_stage[par].ap(),
                                               scale_rep.ap(), ALU.mult))
        P.drain("vector")
        hs = P.emit("vector",
            lambda e, par=par: e.tensor_tensor(h_stage[par].ap(), h_stage[par].ap(),
                                               shift_rep.ap(), ALU.add),
            inc=sm_v)
        if upto <= 2:
            hfm_tok[ts] = hs
            continue
        last_ev = None
        for cb in range(2):
            ps = psA if cb == 0 else psB
            tw = [hs, ps_free[cb], ps_free_s[cb]]
            for i in range(4):
                c = cb * 4 + i
                P.emit("tensor",
                    lambda e, c=c, i=i, par=par, ps=ps: e.transpose(
                        ps.ap()[:, i * 128:(i + 1) * 128],
                        h_stage[par].ap()[:, c * 128:(c + 1) * 128],
                        id32.ap()),
                    waits=tw, inc=(sm_t if i == 3 else None))
                tw = []
            trk = sm_t.tok()
            if cb == 1:
                hst_free[par] = trk
            if upto <= 3:
                last_ev = trk
                ps_free[cb] = None
                hst_free[par] = None
                continue
            last_ev = P.emit("vector",
                lambda e, cb=cb, ts=ts, ps=ps: e.tensor_copy(
                    h_fm.ap().rearrange("p (c t j) -> p c t j", c=NCH, t=NTS)
                    [:, cb * 4:(cb + 1) * 4, ts:ts + 1, :],
                    ps.ap().rearrange("p (c j) -> p c j", c=4).unsqueeze(2)),
                waits=[trk], inc=sm_v)
            ps_free[cb] = last_ev
            if rt_f32 is not None:
                ps_free_s[cb] = P.emit("scalar",
                    lambda e, cb=cb, ps=ps: e.copy(
                        rt_f32.ap()[:, cb * 512:(cb + 1) * 512], ps.ap()),
                    waits=[trk, rtmm_tok, last_ev], inc=sm_s)
        hfm_tok[ts] = last_ev
        if rt_f32 is not None:
            for c in range(NCH):
                P.emit("tensor",
                    lambda e, c=c, ts=ts: e.matmul(
                        logits_ps.ap()[:, ts * 12:(ts + 1) * 12],
                        rt_f32.ap()[:, c * 128:(c + 1) * 128],
                        wr_sb.ap()[:, c * 12:(c + 1) * 12],
                        start=(ts == 0 and c == 0), stop=(ts == NTS - 1 and c == NCH - 1)),
                    waits=[ps_free_s[0], ps_free_s[1]],
                    inc=(sm_t if c == NCH - 1 else None))
            rtmm_tok = sm_t.tok()
    return rtmm_tok if rt_f32 is not None else hfm_tok[NTS - 1]


def build_program(stage=4.0, debug=False):
    nc = bass.Bass("TRN2", target_bir_lowering=False)
    P = Plan()

    di = {}
    def inp(name, shape, dt=F32):
        di[name] = nc.dram_tensor(name, list(shape), dt, kind="ExternalInput").ap()
        return di[name]

    x_d = inp("x", (T, D))
    inp("w_dq", (128, 1024)); inp("w_dqT", (1024, 128))
    inp("w_uq", (1024, 128)); inp("w_uqT", (128, 1024))
    inp("w_dkvT", (1024, 128)); inp("w_uk", (1024, 128))
    inp("w_uv", (1024, 128)); inp("w_oT", (1024, 1024))
    inp("wmod", (1024, 6144))
    inp("c_cols", (128, 8))
    inp("ws1", (1024, 4096))
    inp("ws2", (2048, 1024))
    inp("we1", (NE, 1024, 6144))
    inp("we2", (NE, 3072, 1024))
    inp("wr", (1024, 12))
    inp("br_rep", (128, 12))
    inp("bs1c", (128, 32))
    inp("be1c", (128, NE, 48))
    inp("be2p", (128, 1024))
    inp("bs2rp", (128, 1024))
    inp("ones1", (1, 128))
    inp("onescol", (128, 1))
    inp("epscol", (128, 1))
    inp("ident32", (128, 128))
    inp("identbf", (128, 128), BF16)
    inp("maskbf", (128, 128), BF16)
    out_d = nc.dram_tensor("out", [T, D], F32, kind="ExternalOutput").ap()

    with ExitStack() as top:
        def sballoc(ctx, name, shape, dt=F32):
            return ctx.enter_context(nc.sbuf_tensor(name, list(shape), dt))
        sb = lambda name, shape, dt=F32: sballoc(top, name, shape, dt)

        x_tm = sb("x_tm", (128, NTS * D))
        h_fm = sb("h_fm", (128, NCH * T), BF16)
        gb_rep = sb("gb_rep", (128, D), BF16)
        wr_sb = sb("wr_sb", (128, NCH * 12))
        br_sb = sb("br_sb", (128, 12))
        bs1c_sb = sb("bs1c_sb", (128, 32))
        id32 = sb("id32", (128, 128))
        idbf = sb("idbf", (128, 128), BF16)
        mask_sb = sb("mask_sb", (128, 128), BF16)
        ones1_sb = sb("ones1_sb", (1, 128))
        onescol_sb = sb("onescol_sb", (128, 1))
        epscol_sb = sb("epscol_sb", (128, 1))
        c_cols_sb = sb("c_cols_sb", (128, 8))
        silu_c = sb("silu_c", (128, 8))
        s1c = sb("s1c", (128, NTS)); s2c = sb("s2c", (128, NTS))
        meanc = sb("meanc", (128, NTS)); ex2c = sb("ex2c", (128, NTS))
        nvarc = sb("nvarc", (128, NTS)); lnvc = sb("lnvc", (128, NTS))
        rstdc = sb("rstdc", (128, NTS))
        sq_dump = sb("sq_dump", (128, D), BF16)
        rl = sb("rl", (128, 12)); rm1 = sb("rm1", (128, 1)); rm1n = sb("rm1n", (128, 1))
        req = sb("req", (128, 12)); rl2 = sb("rl2", (128, 12)); rm2 = sb("rm2", (128, 1))
        rex = sb("rex", (128, 12)); rmk = sb("rmk", (128, 12)); rp = sb("rp", (128, 12))
        rZs = sb("rZs", (128, 1)); rrZ = sb("rrZ", (128, 1))

        sm_t = Sem(nc, top, "sm_t")
        sm_v = Sem(nc, top, "sm_v")
        sm_s = Sem(nc, top, "sm_s")
        sm_px = Sem(nc, top, "sm_px")
        sm_st = Sem(nc, top, "sm_st")
        sm_ga = Sem(nc, top, "sm_ga")
        sm_w1 = [Sem(nc, top, "sm_w1a"), Sem(nc, top, "sm_w1b")]
        sm_w2 = [Sem(nc, top, "sm_w2a"), Sem(nc, top, "sm_w2b")]
        sm_wm = [Sem(nc, top, "sm_wma"), Sem(nc, top, "sm_wmb")]
        sm_gb = Sem(nc, top, "sm_gb")
        sm_gc = Sem(nc, top, "sm_gc")
        sm_wo = Sem(nc, top, "sm_wo")

        cols = (s1c, s2c, meanc, ex2c, nvarc, lnvc, rstdc)

        late = top.enter_context(ExitStack())
        w_rt = sballoc(late, "w_rt", (128, NTS * 12))
        wT = sballoc(late, "wTq", (12, T), BF16)
        be2s_bf = sballoc(late, "be2s", (128, D), BF16)
        be1c_sb = sballoc(late, "be1c_sb", (128, NE * 48), BF16)

        mid = ExitStack()
        k_eff = sballoc(mid, "k_eff", (128, 8 * 128), BF16)
        v_eff = sballoc(mid, "v_eff", (128, D), BF16)
        ca1_rep = sballoc(mid, "ca1_rep", (128, D)); sa_rep = sballoc(mid, "sa_rep", (128, D))
        cb1_rep = sballoc(mid, "cb1_rep", (128, D)); sb_rep = sballoc(mid, "sb_rep", (128, D))

        def ld(dst_ap, src_ap, waits=()):
            return P.emit("sync", lambda e, d=dst_ap, s=src_ap: e.dma_start(d, s),
                          waits=waits, inc=sm_px, inck=16)

        ld(id32.ap(), di["ident32"]); ld(idbf.ap(), di["identbf"])
        ld(mask_sb.ap(), di["maskbf"]); ld(ones1_sb.ap(), di["ones1"])
        ld(onescol_sb.ap(), di["onescol"]); ld(c_cols_sb.ap(), di["c_cols"])
        ld(epscol_sb.ap(), di["epscol"])
        ld(br_sb.ap(), di["br_rep"]); ld(bs1c_sb.ap(), di["bs1c"])
        for kc in range(NCH):
            ld(wr_sb.ap()[:, kc * 12:(kc + 1) * 12], di["wr"][kc * 128:(kc + 1) * 128, :])
        for ts in range(NTS):
            ld(x_tm.ap()[:, ts * D:(ts + 1) * D], x_d[ts * 128:(ts + 1) * 128, :])
        px_all = sm_px.tok()
        consts_tok = px_all
        x_tok = [px_all] * NTS

        def gld(dst_ap, src_ap, sem, waits=()):
            return P.emit("gpsimd", lambda e, d=dst_ap, s=src_ap: e.dma_start(d, s),
                          waits=waits, inc=sem, inck=16)

        def chunked(dst_sb, src_ap, nk):
            return gld(dst_sb.ap().rearrange("p (k c) -> p k c", k=nk),
                       src_ap.rearrange("(k p) c -> p k c", p=128), sm_ga)

        be1c_tok = gld(be1c_sb.ap().rearrange("p (e s) -> p e s", e=NE), di["be1c"], sm_gb)

        hfm_tok = [None] * NTS
        with ExitStack() as pre:
            psb = lambda name, shape, dt=F32: sballoc(pre, name, shape, dt)
            w_dq_sb = psb("w_dq_sb", (128, 1024), BF16)
            w_uq_sb = psb("w_uq_sb", (128, 8 * 128), BF16)
            w_uk_sb = psb("w_uk_sb", (128, 8 * 128), BF16)
            w_uv_sb = psb("w_uv_sb", (128, 8 * 128), BF16)
            w_oT_sb = psb("w_oT_sb", (128, 8 * 512), BF16)
            mod_row = psb("mod_row", (1, 6144))
            mrow_g1 = psb("mrow_g1", (1, 1024)); mrow_g2 = psb("mrow_g2", (1, 1024))
            wmodbuf = [psb(f"wmodb{i}", (128, 8 * 256), F32) for i in range(2)]
            ga_rep = psb("ga_rep", (128, D), BF16)
            h_stage = [psb(f"h_stage{i}", (128, D)) for i in range(2)]
            B1_bf = psb("B1_bf", (128, 128), BF16)

            gld(w_dq_sb.ap(), di["w_dq"], sm_ga)
            chunked(w_uq_sb, di["w_uq"], 8)
            chunked(w_uk_sb, di["w_uk"], 8)
            chunked(w_uv_sb, di["w_uv"], 8)
            attw_tok = sm_ga.tok()
            woT_tok0 = gld(w_oT_sb.ap().rearrange("p (k c) -> p k c", k=8),
                           di["w_oT"][:, 0:512].rearrange("(k p) c -> p k c", p=128),
                           sm_wo)

            silu_tok = P.emit("scalar",
                              lambda e: e.activation(silu_c.ap(), c_cols_sb.ap(), AF.Silu),
                              waits=[consts_tok], inc=sm_s)

            stat1_v = []; stat1_s = []
            for ts in range(NTS):
                xs = x_tm.ap()[:, ts * D:(ts + 1) * D]
                stat1_v.append(P.emit("vector",
                    lambda e, xs=xs, ts=ts: e.reduce_sum(s1c.ap()[:, ts:ts + 1], xs, axis=AX.X),
                    waits=[x_tok[ts]], inc=sm_v))
                stat1_s.append(P.emit("scalar",
                    lambda e, xs=xs, ts=ts: e.activation(sq_dump.ap(), xs, AF.Square,
                                                         accum_out=s2c.ap()[:, ts:ts + 1]),
                    waits=[x_tok[ts]], inc=sm_s))

            with nc.psum_tensor("mod_ps", [1, 256], F32) as mod_ps, \
                 nc.psum_tensor("pp1", [128, 512], F32) as pp1, \
                 nc.psum_tensor("pp2", [128, 512], F32) as pp2:

                NMB = 24 if stage >= 0.5 else 0
                mod_evac = [None] * max(NMB, 1)
                wm_cnt = [0, 0]
                wm_tok = {}

                def wmod_dma(j, waits):
                    sl = j % 2
                    wm_cnt[sl] += 1
                    src = di["wmod"][:, j * 256:(j + 1) * 256].rearrange("(k p) c -> p k c", p=128)
                    gld(wmodbuf[sl].ap().rearrange("p (k c) -> p k c", k=8), src,
                        sm_wm[sl], waits=waits)
                    wm_tok[j] = (sm_wm[sl], 16 * wm_cnt[sl])

                wmod_dma(0, [silu_tok])
                wmod_dma(1, [])
                for j in range(NMB):
                    sl = j % 2
                    for kc in range(NCH):
                        P.emit("tensor",
                            lambda e, kc=kc, sl=sl: e.matmul(
                                mod_ps.ap(),
                                silu_c.ap()[:, kc:kc + 1],
                                wmodbuf[sl].ap()[:, kc * 256:(kc + 1) * 256],
                                start=(kc == 0), stop=(kc == NCH - 1)),
                            waits=[wm_tok[j], silu_tok, mod_evac[j - 1] if j >= 1 else None],
                            inc=(sm_t if kc == NCH - 1 else None))
                    mm_tok = sm_t.tok()
                    mod_evac[j] = P.emit("vector",
                        lambda e, j=j: e.tensor_copy(mod_row.ap()[:, j * 256:(j + 1) * 256],
                                                     mod_ps.ap()),
                        waits=[mm_tok], inc=sm_v)
                    if j + 2 < NMB:
                        wmod_dma(j + 2, [mm_tok])
                mod_tok = mod_evac[NMB - 1] if NMB else None

                def vop(fn, waits=(), dr=False):
                    t = P.emit("vector", fn, waits=waits, inc=sm_v)
                    if dr:
                        P.drain("vector")
                    return t
                if stage < 0.6:
                    modp_tok = None
                    reps_tok = None
                    b1_tok = None
                    keff_tok = None
                    veff_tok = None
                if stage >= 0.6: vop(lambda e: e.tensor_scalar(mod_row.ap()[:, 1024:2048],
                                              mod_row.ap()[:, 1024:2048], 1.0, None, ALU.add),
                    waits=[mod_tok])
                vop(lambda e: e.tensor_scalar(mod_row.ap()[:, 4096:5120],
                                              mod_row.ap()[:, 4096:5120], 1.0, None, ALU.add),
                    dr=True)
                for off in ((2048, 5120) if stage >= 0.6 else ()):
                    g = mod_row.ap()[:, off:off + 1024]
                    vop(lambda e, g=g: e.tensor_scalar(mrow_g1.ap(), g, -1.0, None, ALU.mult), dr=True)
                    vop(lambda e, g=g: e.tensor_tensor(mrow_g1.ap(), g, mrow_g1.ap(), ALU.max), dr=True)
                    vop(lambda e: e.tensor_scalar(mrow_g1.ap(), mrow_g1.ap(), 0.25, None, ALU.max))
                    vop(lambda e, g=g: e.tensor_scalar(mrow_g2.ap(), g, 0.0, None, ALU.is_ge), dr=True)
                    vop(lambda e: e.tensor_scalar(mrow_g2.ap(), mrow_g2.ap(), 2.0, -1.0,
                                                  ALU.mult, ALU.add), dr=True)
                    vop(lambda e, g=g: e.tensor_tensor(g, mrow_g1.ap(), mrow_g2.ap(), ALU.mult),
                        dr=True)
                if stage >= 0.6:
                    modp_tok = sm_v.tok()

                reps = ([(sa_rep, 0), (ca1_rep, 1024), (ga_rep, 2048),
                        (sb_rep, 3072), (cb1_rep, 4096), (gb_rep, 5120)]
                        if stage >= 0.6 else [])
                prev_rep_ev = None
                for dst, off in reps:
                    for n2 in range(2):
                        mm = P.emit("tensor",
                            lambda e, off=off, n2=n2: e.matmul(
                                pp1.ap(), ones1_sb.ap(),
                                mod_row.ap()[:, off + n2 * 512: off + (n2 + 1) * 512],
                                start=True, stop=True),
                            waits=[modp_tok, consts_tok, prev_rep_ev], inc=sm_t)
                        prev_rep_ev = P.emit("vector",
                            lambda e, dst=dst, n2=n2: e.tensor_copy(
                                dst.ap()[:, n2 * 512:(n2 + 1) * 512], pp1.ap()),
                            waits=[mm], inc=sm_v)
                if stage >= 0.6:
                    reps_tok = prev_rep_ev

                for kc in (range(NCH) if stage >= 0.8 else ()):
                    P.emit("tensor",
                        lambda e, kc=kc: e.matmul(
                            pp2.ap()[:, 0:128],
                            w_uq_sb.ap()[:, kc * 128:(kc + 1) * 128],
                            w_uk_sb.ap()[:, kc * 128:(kc + 1) * 128],
                            start=(kc == 0), stop=(kc == NCH - 1)),
                        waits=[attw_tok], inc=(sm_t if kc == NCH - 1 else None))
                if stage >= 0.8:
                    b1_mm = sm_t.tok()
                    b1_tok = P.emit("vector",
                        lambda e: e.tensor_copy(B1_bf.ap(), pp2.ap()[:, 0:128]),
                        waits=[b1_mm], inc=sm_v)
                keff_ev = None
                for s in (range(NCH) if stage >= 0.8 else ()):
                    mm = P.emit("tensor",
                        lambda e, s=s: e.matmul(
                            pp2.ap()[:, 128:256],
                            w_dq_sb.ap()[:, s * 128:(s + 1) * 128],
                            B1_bf.ap(), start=True, stop=True),
                        waits=[b1_tok, keff_ev], inc=sm_t)
                    keff_ev = P.emit("vector",
                        lambda e, s=s: e.tensor_copy(k_eff.ap()[:, s * 128:(s + 1) * 128],
                                                     pp2.ap()[:, 128:256]),
                        waits=[mm], inc=sm_v)
                if stage >= 0.8:
                    keff_tok = keff_ev

                veff_ev = prev_rep_ev
                woT_tok = [woT_tok0, None]
                for n2 in (range(2) if stage >= 0.8 else ()):
                    for kc in range(NCH):
                        P.emit("tensor",
                            lambda e, kc=kc: e.matmul(
                                pp1.ap(),
                                w_uv_sb.ap()[:, kc * 128:(kc + 1) * 128],
                                w_oT_sb.ap()[:, kc * 512:(kc + 1) * 512],
                                start=(kc == 0), stop=(kc == NCH - 1)),
                            waits=[attw_tok, woT_tok[n2], reps_tok, veff_ev],
                            inc=(sm_t if kc == NCH - 1 else None))
                    mm = sm_t.tok()
                    if n2 == 0:
                        woT_tok[1] = gld(
                            w_oT_sb.ap().rearrange("p (k c) -> p k c", k=8),
                            di["w_oT"][:, 512:1024].rearrange("(k p) c -> p k c", p=128),
                            sm_wo, waits=[mm])
                    veff_ev = P.emit("vector",
                        lambda e, n2=n2: e.tensor_tensor(
                            v_eff.ap()[:, n2 * 512:(n2 + 1) * 512], pp1.ap(),
                            ga_rep.ap()[:, n2 * 512:(n2 + 1) * 512], ALU.mult),
                        waits=[mm], inc=sm_v)
                if stage >= 0.8:
                    veff_tok = veff_ev

            with nc.psum_tensor("lnpA", [128, 512], F32) as lnpA, \
                 nc.psum_tensor("lnpB", [128, 512], F32) as lnpB:
                if stage >= 0.9:
                    ln_chain(P, x_tm, cols, ca1_rep, sa_rep, h_stage, (lnpA, lnpB), id32,
                             h_fm, hfm_tok, stat1_v, stat1_s, reps_tok, sm_t, sm_v, sm_s,
                             epscol_sb)

        def partial_store(toks):
            for ts in range(NTS):
                P.emit("sync",
                    lambda e_, ts=ts: e_.dma_start(
                        out_d[ts * 128:(ts + 1) * 128, :],
                        x_tm.ap()[:, ts * D:(ts + 1) * D]),
                    waits=[toks[ts] if isinstance(toks, list) else toks,
                           hfm_tok[NTS - 1]] if hfm_tok[NTS - 1] else
                          [toks[ts] if isinstance(toks, list) else toks],
                    inc=sm_st, inck=16)
            P.emit("sync", lambda e: None, waits=[(sm_st, sm_st.n)])

        def finish():
            with nc.Block() as blk:
                blk.tensor(lambda e: [op(e) for op in P.ops["tensor"]])
                blk.vector(lambda e: [op(e) for op in P.ops["vector"]])
                blk.scalar(lambda e: [op(e) for op in P.ops["scalar"]])
                blk.sync(lambda e: [op(e) for op in P.ops["sync"]])
                blk.gpsimd(lambda e: [op(e) for op in P.ops["gpsimd"]])

        if stage <= 1.0:
            partial_store(x_tok)
            mid.close()
            finish()
            return nc

        xadd_tok = [None] * NTS
        with ExitStack() as att:
            asb = lambda name, shape, dt=BF16: sballoc(att, name, shape, dt)
            w_dqT_sb = asb("w_dqT_sb", (128, 8 * 128))
            w_uqT_sb = asb("w_uqT_sb", (128, 1024))
            w_dkvT_sb = asb("w_dkvT_sb", (128, 8 * 128))
            q_fm = asb("q_fm", (128, NCH * T))
            ckv_fm = asb("ckv_fm", (128, T))
            ckv_tm = asb("ckv_tm", (128, NTS * 128))
            gld(w_dqT_sb.ap().rearrange("p (k c) -> p k c", k=8),
                di["w_dqT"].rearrange("(k p) c -> p k c", p=128), sm_gc,
                waits=[veff_tok])
            gld(w_uqT_sb.ap(), di["w_uqT"], sm_gc)
            gld(w_dkvT_sb.ap().rearrange("p (k c) -> p k c", k=8),
                di["w_dkvT"].rearrange("(k p) c -> p k c", p=128), sm_gc)
            attw2_tok = sm_gc.tok()

            with ExitStack() as apre:
                q1_sb = sballoc(apre, "q1_sb", (128, T), BF16)
                with nc.psum_tensor("ap1", [128, 512], F32) as ap1, \
                     nc.psum_tensor("ap2", [128, 512], F32) as ap2:
                    ev1 = None
                    for tg in range(4):
                        for kc in range(NCH):
                            P.emit("tensor",
                                lambda e, kc=kc, tg=tg: e.matmul(
                                    ap1.ap(),
                                    w_dkvT_sb.ap()[:, kc * 128:(kc + 1) * 128],
                                    h_fm.ap()[:, kc * T + tg * 512: kc * T + (tg + 1) * 512],
                                    start=(kc == 0), stop=(kc == NCH - 1)),
                                waits=[attw2_tok, hfm_tok[4 * tg + 3], ev1],
                                inc=(sm_t if kc == NCH - 1 else None))
                        mm = sm_t.tok()
                        ev1 = P.emit("vector",
                            lambda e, tg=tg: e.tensor_copy(
                                ckv_fm.ap()[:, tg * 512:(tg + 1) * 512], ap1.ap()),
                            waits=[mm], inc=sm_v)
                    ev2 = None
                    for tg in range(4):
                        for kc in range(NCH):
                            P.emit("tensor",
                                lambda e, kc=kc, tg=tg: e.matmul(
                                    ap2.ap(),
                                    w_dqT_sb.ap()[:, kc * 128:(kc + 1) * 128],
                                    h_fm.ap()[:, kc * T + tg * 512: kc * T + (tg + 1) * 512],
                                    start=(kc == 0), stop=(kc == NCH - 1)),
                                waits=[ev2], inc=(sm_t if kc == NCH - 1 else None))
                        mm = sm_t.tok()
                        ev2 = P.emit("vector",
                            lambda e, tg=tg: e.tensor_copy(
                                q1_sb.ap()[:, tg * 512:(tg + 1) * 512], ap2.ap()),
                            waits=[mm], inc=sm_v)
                    q1_tok = ev2
                    ev3 = None
                    for ts in range(NTS):
                        for kc in range(NCH):
                            P.emit("tensor",
                                lambda e, kc=kc, ts=ts: e.matmul(
                                    ap1.ap()[:, 0:128],
                                    h_fm.ap()[:, kc * T + ts * 128: kc * T + (ts + 1) * 128],
                                    w_dkvT_sb.ap()[:, kc * 128:(kc + 1) * 128],
                                    start=(kc == 0), stop=(kc == NCH - 1)),
                                waits=[ev3], inc=(sm_t if kc == NCH - 1 else None))
                        mm = sm_t.tok()
                        ev3 = P.emit("vector",
                            lambda e, ts=ts: e.tensor_copy(
                                ckv_tm.ap()[:, ts * 128:(ts + 1) * 128], ap1.ap()[:, 0:128]),
                            waits=[mm], inc=sm_v)
                    ckvtm_tok = ev3
                    ev4 = None
                    for s in range(NCH):
                        for tg in range(4):
                            mm = P.emit("tensor",
                                lambda e, s=s, tg=tg: e.matmul(
                                    ap2.ap(),
                                    w_uqT_sb.ap()[:, s * 128:(s + 1) * 128],
                                    q1_sb.ap()[:, tg * 512:(tg + 1) * 512],
                                    start=True, stop=True),
                                waits=[q1_tok, ev4], inc=sm_t)
                            ev4 = P.emit("vector",
                                lambda e, s=s, tg=tg: e.tensor_copy(
                                    q_fm.ap()[:, s * T + tg * 512: s * T + (tg + 1) * 512],
                                    ap2.ap()),
                                waits=[mm], inc=sm_v)
                    qfm_tok = ev4

            ql_bf2 = [asb("ql_bfA", (128, 512)), asb("ql_bfB", (128, 512))]
            P_bf = [asb(f"P_bf{i}", (128, T)) for i in range(2)]
            PT_bf = asb("PT_bf", (128, 16 * 128))
            ctx_bf = [asb(f"ctx_bf{i}", (128, 128)) for i in range(2)]
            y_fm = asb("y_fm", (128, 8 * 512))
            m1col = sballoc(att, "m1col", (128, 8))
            lcol = sballoc(att, "lcol", (128, 8))
            rcol = sballoc(att, "rcol", (128, 8))

            with nc.psum_tensor("S_ps", [128, 2048], F32) as S_ps, \
                 nc.psum_tensor("ql_ps", [128, 512], F32) as ql_ps, \
                 nc.psum_tensor("PT_ps", [128, 1024], BF16) as PT_ps, \
                 nc.psum_tensor("ctx_ps", [128, 512], F32) as ctx_ps, \
                 nc.psum_tensor("y_ps", [128, 512], F32) as y_ps:

                st_ = {
                    "exp": [None, None],
                    "ptd": [None, None],
                    "qlev": [None, None],
                    "ctxev": [None, None],
                    "ptps": None,
                    "yfmev": None,
                    "ymm": None,
                    "smm": {}, "mx": {}, "pn": {},
                }

                def ph_ql(h, g, buf):
                    hp = h // 2
                    po = 64 * (h % 2)
                    mm = P.emit("tensor",
                        lambda e, hp=hp, po=po, g=g: e.matmul(
                            ql_ps.ap(),
                            k_eff.ap()[po:po + 64, hp * 128:(hp + 1) * 128],
                            q_fm.ap()[po:po + 64, hp * T + g * 512: hp * T + (g + 1) * 512],
                            start=True, stop=True),
                        waits=[qfm_tok, keff_tok, st_["qlev"][0], st_["qlev"][1]],
                        inc=sm_t)
                    st_["qlev"][buf] = P.emit("vector",
                        lambda e, buf=buf: e.tensor_copy(ql_bf2[buf].ap(), ql_ps.ap()),
                        waits=[mm], inc=sm_v)

                def ph_S(h, qt, buf, sbase, exw):
                    qtl = qt % 4
                    kmax = 128 * (qt + 1)
                    s_waits = [st_["qlev"][buf]] + exw
                    for b in range((kmax + 511) // 512):
                        lo = b * 512
                        hi = min(kmax, lo + 512)
                        lastb = (hi == kmax)
                        P.emit("tensor",
                            lambda e, lo=lo, hi=hi, qtl=qtl, buf=buf, sbase=sbase: e.matmul(
                                S_ps.ap()[:, sbase + lo: sbase + hi],
                                ql_bf2[buf].ap()[:, qtl * 128:(qtl + 1) * 128],
                                ckv_fm.ap()[:, lo:hi],
                                start=True, stop=(not lastb)),
                            waits=s_waits, inc=None)
                        s_waits = []
                    st_["smm"][(h, qt)] = P.emit("tensor",
                        lambda e, qt=qt, kmax=kmax, sbase=sbase: e.matmul(
                            S_ps.ap()[:, sbase + qt * 128: sbase + kmax],
                            idbf.ap(), mask_sb.ap(),
                            start=False, stop=True),
                        waits=[], inc=sm_t)

                def ph_max(h, qt, sbase, col):
                    kmax = 128 * (qt + 1)
                    P.emit("vector",
                        lambda e, kmax=kmax, col=col, sbase=sbase: e.reduce_max(
                            m1col.ap()[:, col:col + 1],
                            S_ps.ap()[:, sbase: sbase + kmax], axis=AX.X),
                        waits=[st_["smm"][(h, qt)]])
                    P.drain("vector")
                    st_["mx"][(h, qt)] = P.emit("vector",
                        lambda e, col=col: e.tensor_scalar(
                            m1col.ap()[:, col:col + 1], m1col.ap()[:, col:col + 1],
                            -0.125, None, ALU.mult),
                        inc=sm_v)
                    P.drain("vector")

                def ph_exp(h, qt, buf, sbase, col):
                    kmax = 128 * (qt + 1)
                    st_["exp"][buf] = P.emit("scalar",
                        lambda e, kmax=kmax, buf=buf, col=col, sbase=sbase: e.activation(
                            P_bf[buf].ap()[:, 0:kmax],
                            S_ps.ap()[:, sbase: sbase + kmax], AF.Exp,
                            bias=m1col.ap()[:, col:col + 1], scale=0.125,
                            accum_out=lcol.ap()[:, col:col + 1]),
                        waits=[st_["mx"][(h, qt)], st_["ptd"][buf]], inc=sm_s)

                def ph_pn(h, qt, buf, col):
                    kmax = 128 * (qt + 1)
                    P.emit("vector",
                        lambda e, col=col: e.reciprocal(
                            rcol.ap()[:, col:col + 1], lcol.ap()[:, col:col + 1]),
                        waits=[st_["exp"][buf]])
                    P.drain("vector")
                    st_["pn"][(h, qt)] = P.emit("vector",
                        lambda e, kmax=kmax, buf=buf, col=col: e.tensor_scalar(
                            P_bf[buf].ap()[:, 0:kmax], P_bf[buf].ap()[:, 0:kmax],
                            rcol.ap()[:, col:col + 1], None, ALU.mult),
                        inc=sm_v)

                def ph_ptcy(h, qt, buf, ptbase, first_y):
                    qtl = qt % 4
                    po = 64 * (h % 2)
                    nsc = qt + 1
                    first_ctx = True
                    for scb in range(0, nsc, 4):
                        scs = list(range(scb, min(nsc, scb + 4)))
                        tw = [st_["pn"][(h, qt)], st_["ptps"]]
                        for i, sc in enumerate(scs):
                            P.emit("tensor",
                                lambda e, sc=sc, i=i, buf=buf: e.transpose(
                                    PT_ps.ap()[:, i * 128:(i + 1) * 128],
                                    P_bf[buf].ap()[:, sc * 128:(sc + 1) * 128],
                                    idbf.ap()),
                                waits=tw, inc=(sm_t if i == len(scs) - 1 else None))
                            tw = []
                        trk = sm_t.tok()
                        ptb_evac = P.emit("scalar",
                            lambda e, n=len(scs), scb=scb, ptbase=ptbase: e.copy(
                                PT_bf.ap()[:, ptbase + scb * 128: ptbase + (scb + n) * 128],
                                PT_ps.ap()[:, 0:n * 128]),
                            waits=[trk], inc=sm_s)
                        st_["ptps"] = ptb_evac
                        for sc in scs:
                            cw = [ptb_evac]
                            if first_ctx:
                                cw += [ckvtm_tok, st_["ctxev"][0], st_["ctxev"][1]]
                                first_ctx = False
                            P.emit("tensor",
                                lambda e, sc=sc, nsc=nsc, ptbase=ptbase: e.matmul(
                                    ctx_ps.ap()[:, 0:128],
                                    ckv_tm.ap()[:, sc * 128:(sc + 1) * 128],
                                    PT_bf.ap()[:, ptbase + sc * 128: ptbase + (sc + 1) * 128],
                                    start=(sc == 0), stop=(sc == nsc - 1)),
                                waits=cw, inc=(sm_t if sc == nsc - 1 else None))
                    st_["ptd"][buf] = sm_t.tok()
                    ctx_mm = sm_t.tok()
                    st_["ctxev"][buf] = P.emit("vector",
                        lambda e, buf=buf: e.tensor_copy(ctx_bf[buf].ap(),
                                                         ctx_ps.ap()[:, 0:128]),
                        waits=[ctx_mm], inc=sm_v)
                    st_["ymm"] = P.emit("tensor",
                        lambda e, po=po, qtl=qtl, h=h, buf=buf: e.matmul(
                            y_ps.ap()[po:po + 64, qtl * 128:(qtl + 1) * 128],
                            v_eff.ap()[:, h * 64:(h + 1) * 64],
                            ctx_bf[buf].ap(),
                            start=True, stop=True),
                        waits=[st_["ctxev"][buf], veff_tok,
                               st_["yfmev"] if first_y else None],
                        inc=sm_t)

                for g in range(NG):
                    if g < 2:
                        for hp in range(8):
                            h0, h1 = 2 * hp, 2 * hp + 1
                            ph_ql(h0, g, 0)
                            ph_ql(h1, g, 1)
                            for qtl in range(4):
                                qt = 4 * g + qtl
                                ph_S(h0, qt, 0, 0, [st_["exp"][0]])
                                ph_S(h1, qt, 1, 1024, [st_["exp"][1]])
                                if qtl > 0:
                                    ph_ptcy(h0, qt - 1, 0, 0, qtl == 1)
                                    ph_ptcy(h1, qt - 1, 1, 1024, False)
                                ph_max(h0, qt, 0, qtl)
                                ph_max(h1, qt, 1024, 4 + qtl)
                                ph_exp(h0, qt, 0, 0, qtl)
                                ph_exp(h1, qt, 1, 1024, 4 + qtl)
                                ph_pn(h0, qt, 0, qtl)
                                ph_pn(h1, qt, 1, 4 + qtl)
                            ph_ptcy(h0, 4 * g + 3, 0, 0, False)
                            ph_ptcy(h1, 4 * g + 3, 1, 1024, False)
                            st_["yfmev"] = P.emit("vector",
                                lambda e, hp=hp: e.tensor_copy(
                                    y_fm.ap()[:, hp * 512:(hp + 1) * 512], y_ps.ap()),
                                waits=[st_["ymm"]], inc=sm_v)
                    else:
                        for h in range(NH):
                            hp = h // 2
                            buf_prev = None
                            ph_ql(h, g, h % 2)
                            for qtl in range(4):
                                qt = 4 * g + qtl
                                buf = qtl % 2
                                ph_S(h, qt, h % 2, 0, [st_["exp"][0], st_["exp"][1]])
                                if qtl > 0:
                                    ph_ptcy(h, qt - 1, (qtl - 1) % 2, 0,
                                            (h % 2 == 0 and qtl == 1))
                                ph_max(h, qt, 0, qtl)
                                ph_exp(h, qt, buf, 0, qtl)
                                ph_pn(h, qt, buf, qtl)
                            ph_ptcy(h, 4 * g + 3, 1, 0, False)
                            if h % 2 == 1:
                                st_["yfmev"] = P.emit("vector",
                                    lambda e, hp=hp: e.tensor_copy(
                                        y_fm.ap()[:, hp * 512:(hp + 1) * 512], y_ps.ap()),
                                    waits=[st_["ymm"]], inc=sm_v)
                    for j in range(4):
                        ts = 4 * g + j
                        prev_add = None
                        for cb in range(2):
                            tw = [st_["yfmev"], st_["ptps"], prev_add]
                            for i in range(4):
                                c = cb * 4 + i
                                P.emit("tensor",
                                    lambda e, c=c, i=i, j=j: e.transpose(
                                        PT_ps.ap()[:, i * 128:(i + 1) * 128],
                                        y_fm.ap()[:, c * 512 + j * 128: c * 512 + (j + 1) * 128],
                                        idbf.ap()),
                                    waits=tw, inc=(sm_t if i == 3 else None))
                                tw = []
                            trk = sm_t.tok()
                            prev_add = P.emit("vector",
                                lambda e, ts=ts, cb=cb: e.tensor_tensor(
                                    x_tm.ap()[:, ts * D + cb * 512: ts * D + (cb + 1) * 512],
                                    x_tm.ap()[:, ts * D + cb * 512: ts * D + (cb + 1) * 512],
                                    PT_ps.ap()[:, 0:512], ALU.add),
                                waits=[trk], inc=sm_v)
                            st_["ptps"] = prev_add
                        xadd_tok[ts] = prev_add

                if debug:
                    allw = [xadd_tok[NTS - 1], (sm_v, sm_v.n), (sm_s, sm_s.n),
                            (sm_t, sm_t.n)]
                    def dbg_store(name, sbuf, shape, dt):
                        dd = nc.dram_tensor("dbg_" + name, list(shape), dt,
                                            kind="ExternalOutput").ap()
                        P.emit("sync",
                               lambda e, dd=dd, a=sbuf.ap(): e.dma_start(dd, a),
                               waits=allw, inc=sm_st, inck=16)
                    dbg_store("h_fm", h_fm, (128, NCH * T), BF16)
                    dbg_store("ckv_fm", ckv_fm, (128, T), BF16)
                    dbg_store("ckv_tm", ckv_tm, (128, NTS * 128), BF16)
                    dbg_store("q_fm", q_fm, (128, NCH * T), BF16)
                    dbg_store("k_eff", k_eff, (128, 8 * 128), BF16)
                    dbg_store("v_eff", v_eff, (128, D), BF16)
                    dbg_store("P_bf", P_bf[1], (128, T), BF16)
                    dbg_store("PT_bf", PT_bf, (128, 16 * 128), BF16)
                    dbg_store("y_fm", y_fm, (128, 8 * 512), BF16)
                    dbg_store("ca1", ca1_rep, (128, D), F32)
                    dbg_store("sa", sa_rep, (128, D), F32)

        if stage <= 2:
            partial_store(xadd_tok)
            mid.close()
            finish()
            return nc

        hfm2_tok = [None] * NTS
        with ExitStack() as ln2s:
            rt_f32 = sballoc(ln2s, "rt_f32", (128, D))
            h_stage2 = [sballoc(ln2s, f"h2st{i}", (128, D)) for i in range(2)]
            be2_f = sballoc(ln2s, "be2_f", (128, 1024))
            bs2r_f = sballoc(ln2s, "bs2r_f", (128, 1024))
            if stage >= 2.508:
                t1 = ld(be2_f.ap(), di["be2p"], waits=[xadd_tok[NTS - 1]])
                t2 = ld(bs2r_f.ap(), di["bs2rp"])
                if stage >= 2.5085 or stage >= 2.52:
                    be2s_tok = P.emit("vector",
                        lambda e: e.tensor_tensor(be2s_bf.ap(), be2_f.ap(), bs2r_f.ap(), ALU.add),
                        waits=[t1, t2], inc=sm_v)
                else:
                    be2s_tok = t2
            else:
                be2s_tok = None
            with nc.psum_tensor("l2A", [128, 512], F32) as l2A, \
                 nc.psum_tensor("l2B", [128, 512], F32) as l2B, \
                 nc.psum_tensor("logits_ps", [128, NTS * 12], F32) as logits_ps:
                stat2_v = []; stat2_s = []
                for ts in range(NTS):
                    xs = x_tm.ap()[:, ts * D:(ts + 1) * D]
                    if stage >= 2.515:
                        stat2_v.append(P.emit("vector",
                            lambda e, xs=xs, ts=ts: e.reduce_sum(s1c.ap()[:, ts:ts + 1], xs, axis=AX.X),
                            waits=[xadd_tok[ts]], inc=sm_v))
                    else:
                        stat2_v.append(xadd_tok[ts])
                    if stage >= 2.518:
                        stat2_s.append(P.emit("scalar",
                            lambda e, xs=xs, ts=ts: e.activation(sq_dump.ap(), xs, AF.Square,
                                                                 accum_out=s2c.ap()[:, ts:ts + 1]),
                            waits=[xadd_tok[ts]], inc=sm_s))
                    else:
                        stat2_s.append(xadd_tok[ts])
                upto_map = {2.505: 0, 2.508: 0, 2.5085: 0, 2.51: 0, 2.515: 0, 2.518: 0, 2.52: 1, 2.55: 2, 2.58: 3}
                ln2_upto = upto_map.get(stage, 4)
                use_rt = stage >= 2.7
                logits_done = ln_chain(P, x_tm, cols, cb1_rep, sb_rep, h_stage2,
                                       (l2A, l2B), id32, h_fm, hfm2_tok,
                                       stat2_v, stat2_s, reps_tok, sm_t, sm_v, sm_s,
                                       epscol_sb,
                                       rt_f32=(rt_f32 if use_rt else None),
                                       wr_sb=wr_sb, logits_ps=logits_ps,
                                       upto=ln2_upto)

                wt_tok = None
                for ts in (range(NTS) if stage >= 2.9 else ()):
                    P.emit("vector",
                        lambda e, ts=ts: e.tensor_tensor(
                            rl.ap(), logits_ps.ap()[:, ts * 12:(ts + 1) * 12],
                            br_sb.ap(), ALU.add),
                        waits=[logits_done])
                    P.drain("vector")
                    P.emit("vector", lambda e: e.reduce_max(rm1.ap(), rl.ap(), axis=AX.X))
                    P.drain("vector")
                    P.emit("vector",
                           lambda e: e.tensor_scalar(rm1n.ap(), rm1.ap(), -1.0 / TAU, None, ALU.mult))
                    P.emit("vector",
                           lambda e: e.tensor_scalar(req.ap(), rl.ap(), rm1.ap(), None, ALU.is_equal))
                    P.drain("vector")
                    P.emit("vector",
                           lambda e: e.scalar_tensor_tensor(rl2.ap(), req.ap(), -1e30, rl.ap(),
                                                            ALU.mult, ALU.add))
                    P.drain("vector")
                    P.emit("vector", lambda e: e.reduce_max(rm2.ap(), rl2.ap(), axis=AX.X))
                    P.drain("vector")
                    mk = P.emit("vector",
                        lambda e: e.tensor_scalar(rmk.ap(), rl.ap(), rm2.ap(), None, ALU.is_ge),
                        inc=sm_v)
                    ex = P.emit("scalar",
                        lambda e: e.activation(rex.ap(), rl.ap(), AF.Exp,
                                               bias=rm1n.ap(), scale=1.0 / TAU),
                        waits=[mk], inc=sm_s)
                    P.emit("vector",
                           lambda e: e.tensor_tensor(rp.ap(), rex.ap(), rmk.ap(), ALU.mult),
                           waits=[ex])
                    P.drain("vector")
                    P.emit("vector", lambda e: e.reduce_sum(rZs.ap(), rp.ap(), axis=AX.X))
                    P.drain("vector")
                    P.emit("vector", lambda e: e.reciprocal(rrZ.ap(), rZs.ap()))
                    P.drain("vector")
                    wv = P.emit("vector",
                        lambda e, ts=ts: e.tensor_scalar(
                            w_rt.ap()[:, ts * 12:(ts + 1) * 12], rp.ap(), rrZ.ap(),
                            None, ALU.mult),
                        inc=sm_v)
                    P.drain("vector")
                    wtm = P.emit("tensor",
                        lambda e, ts=ts: e.transpose(
                            l2A.ap()[0:12, 0:128],
                            w_rt.ap()[:, ts * 12:(ts + 1) * 12], id32.ap()),
                        waits=[wv, wt_tok], inc=sm_t)
                    wt_tok = P.emit("vector",
                        lambda e, ts=ts: e.tensor_copy(
                            wT.ap()[:, ts * 128:(ts + 1) * 128], l2A.ap()[0:12, 0:128]),
                        waits=[wtm], inc=sm_v)
                routing_tok = wt_tok if stage >= 2.9 else logits_done

        if stage <= 3.59:
            partial_store(routing_tok)
            mid.close()
            finish()
            return nc

        mid.close()

        with ExitStack() as ffn:
            fsb = lambda name, shape, dt=BF16: sballoc(ffn, name, shape, dt)
            act = fsb("act", (128, 24 * TT))
            T_acc = fsb("T_acc", (128, NTSH * D))
            W1buf = [fsb(f"W1buf{i}", (128, 8 * 256)) for i in range(2)]
            W2buf = [fsb(f"W2buf{i}", (128, 24 * 128)) for i in range(2)]
            yE0 = fsb("yE_sb0", (128, 4 * TT))
            yE_sb = [yE0, yE0]
            gg_sb = [fsb(f"gg{i}", (128, TT)) for i in range(2)]

            def npairs(e):
                return 16 if e == 0 else 24

            def nkc2(e):
                return 16 if e == 0 else 24

            hfm_all = hfm2_tok[NTS - 1]
            final_tok_prev = [None] * NTSH

            with nc.psum_tensor("Pg", [128, TT], F32) as Pg, \
                 nc.psum_tensor("Pa", [128, TT], F32) as Pa, \
                 nc.psum_tensor("yE_ps", [128, TT], F32) as yE_ps, \
                 nc.psum_tensor("tr_psA", [128, 1024], BF16) as tr_psA, \
                 nc.psum_tensor("tr_psB", [128, 1024], BF16) as tr_psB:
                tr_ps2 = [tr_psA, tr_psB]

                w1_cnt = [0, 0]
                w2_cnt = [0, 0]
                w1_slot_use = [None, None]
                w2_slot_use = [None, None]
                gelu_tok = None
                prev_stt = None
                yE_ev_v = None
                yE_ev_s = None
                yslot_free = [None, None]
                trhalf_tok = [None, None]

                for th in range(2):
                    toff = th * TT
                    tinit = [None] * NTSH
                    prev_cp = None
                    for ts in range(NTSH):
                        gts = th * NTSH + ts
                        for n2 in range(2):
                            P.emit("tensor",
                                lambda e, gts=gts, n2=n2: e.matmul(
                                    yE_ps.ap()[:, n2 * 512:(n2 + 1) * 512],
                                    wT.ap()[:, gts * 128:(gts + 1) * 128],
                                    be2s_bf.ap()[0:12, n2 * 512:(n2 + 1) * 512],
                                    start=True, stop=True),
                                waits=[routing_tok, be2s_tok, prev_cp, yE_ev_v, yE_ev_s],
                                inc=(sm_t if n2 == 1 else None))
                        mm = sm_t.tok()
                        prev_cp = P.emit("scalar",
                            lambda e, ts=ts: e.copy(
                                T_acc.ap()[:, ts * D:(ts + 1) * D], yE_ps.ap()),
                            waits=[mm, final_tok_prev[ts]], inc=sm_s)
                        tinit[ts] = prev_cp
                    yep_free = prev_cp

                    w1_plan = [(e, pr) for e in range(13) for pr in range(npairs(e))]
                    w2_plan = [(e, fs) for e in range(13) for fs in range(8)]
                    w1_tok = {}
                    w2_tok = {}
                    w1_i = [0]
                    w2_i = [0]

                    def w1_issue():
                        if w1_i[0] >= len(w1_plan):
                            return
                        e, pr = w1_plan[w1_i[0]]
                        sl = w1_i[0] % 2
                        w1_cnt[sl] += 1
                        src = (di["ws1"][:, pr * 256:(pr + 1) * 256] if e == 0
                               else di["we1"][e - 1, :, pr * 256:(pr + 1) * 256])
                        gld(W1buf[sl].ap().rearrange("p (k c) -> p k c", k=8),
                            src.rearrange("(k p) c -> p k c", p=128),
                            sm_w1[sl], waits=[w1_slot_use[sl], routing_tok])
                        w1_tok[(e, pr)] = ((sm_w1[sl], 16 * w1_cnt[sl]), sl)
                        w1_i[0] += 1

                    def w2_issue():
                        if w2_i[0] >= len(w2_plan):
                            return
                        e, fs = w2_plan[w2_i[0]]
                        sl = w2_i[0] % 2
                        kk = nkc2(e)
                        src = (di["ws2"][:, fs * 128:(fs + 1) * 128] if e == 0
                               else di["we2"][e - 1, :, fs * 128:(fs + 1) * 128])
                        src3 = src.rearrange("(k p) c -> p k c", p=128)
                        gate = [w2_slot_use[sl], routing_tok]
                        for k0 in range(0, kk, 8):
                            w2_cnt[sl] += 1
                            gld(W2buf[sl].ap()[:, k0 * 128:(k0 + 8) * 128]
                                .rearrange("p (k c) -> p k c", k=8),
                                src3[:, k0:k0 + 8, :],
                                sm_w2[sl], waits=gate)
                            gate = []
                        w2_tok[(e, fs)] = ((sm_w2[sl], 16 * w2_cnt[sl]), sl)
                        w2_i[0] += 1

                    w1_issue(); w1_issue()
                    w2_issue(); w2_issue()

                    for e in range(13 if stage >= 3.4 else 0):
                        np_ = npairs(e)
                        kk = nkc2(e)
                        stt_last = None
                        for pr in range(np_):
                            dtok, sl = w1_tok[(e, pr)]
                            par = pr % 2
                            first_w = [dtok, hfm_all, gelu_tok]
                            for kc in range(NCH):
                                for tt2 in range(2):
                                    P.emit("tensor",
                                        lambda e_, sl=sl, kc=kc, tt2=tt2, toff=toff: e_.matmul(
                                            Pg.ap()[:, tt2 * 512:(tt2 + 1) * 512],
                                            W1buf[sl].ap()[:, kc * 256 + 128: kc * 256 + 256],
                                            h_fm.ap()[:, kc * T + toff + tt2 * 512:
                                                      kc * T + toff + (tt2 + 1) * 512],
                                            start=(kc == 0), stop=(kc == NCH - 1)),
                                        waits=first_w,
                                        inc=(sm_t if (kc == NCH - 1 and tt2 == 1) else None))
                                    first_w = []
                            gmm = sm_t.tok()
                            bcol = (bs1c_sb.ap()[:, 2 * pr + 1: 2 * pr + 2] if e == 0
                                    else be1c_sb.ap()[:, (e - 1) * 48 + 2 * pr + 1:
                                                      (e - 1) * 48 + 2 * pr + 2])
                            gelu_tok = P.emit("scalar",
                                lambda e_, par=par, bcol=bcol: e_.activation(
                                    gg_sb[par].ap(), Pg.ap(), AF.Gelu, bias=bcol),
                                waits=[gmm, be1c_tok], inc=sm_s)
                            first_w = [prev_stt]
                            for kc in range(NCH):
                                for tt2 in range(2):
                                    P.emit("tensor",
                                        lambda e_, sl=sl, kc=kc, tt2=tt2, toff=toff: e_.matmul(
                                            Pa.ap()[:, tt2 * 512:(tt2 + 1) * 512],
                                            W1buf[sl].ap()[:, kc * 256: kc * 256 + 128],
                                            h_fm.ap()[:, kc * T + toff + tt2 * 512:
                                                      kc * T + toff + (tt2 + 1) * 512],
                                            start=(kc == 0), stop=(kc == NCH - 1)),
                                        waits=first_w,
                                        inc=(sm_t if (kc == NCH - 1 and tt2 == 1) else None))
                                    first_w = []
                            amm = sm_t.tok()
                            w1_slot_use[sl] = amm
                            acol = (bs1c_sb.ap()[:, 2 * pr: 2 * pr + 1] if e == 0
                                    else be1c_sb.ap()[:, (e - 1) * 48 + 2 * pr:
                                                      (e - 1) * 48 + 2 * pr + 1])
                            prev_stt = P.emit("vector",
                                lambda e_, pr=pr, par=par, acol=acol: e_.scalar_tensor_tensor(
                                    act.ap()[:, pr * TT:(pr + 1) * TT],
                                    Pa.ap(), acol, gg_sb[par].ap(),
                                    ALU.add, ALU.mult),
                                waits=[gelu_tok, amm], inc=sm_v)
                            stt_last = prev_stt
                            w1_issue()
                        for fs in range(8 if stage >= 3.6 else 0):
                            dtok2, sl2 = w2_tok[(e, fs)]
                            yslot = (fs // 4) % 2
                            first_w = [dtok2, yE_ev_v, yE_ev_s, stt_last, yep_free]
                            if fs % 4 == 0:
                                first_w.append(yslot_free[yslot])
                            for kc in range(kk):
                                for tt2 in range(2):
                                    P.emit("tensor",
                                        lambda e_, sl2=sl2, kc=kc, tt2=tt2: e_.matmul(
                                            yE_ps.ap()[:, tt2 * 512:(tt2 + 1) * 512],
                                            W2buf[sl2].ap()[:, kc * 128:(kc + 1) * 128],
                                            act.ap()[:, kc * TT + tt2 * 512:
                                                     kc * TT + (tt2 + 1) * 512],
                                            start=(kc == 0), stop=(kc == kk - 1)),
                                        waits=first_w,
                                        inc=(sm_t if (kc == kk - 1 and tt2 == 1) else None))
                                    first_w = []
                            ymm2 = sm_t.tok()
                            w2_slot_use[sl2] = ymm2
                            yE_ev_v = P.emit("vector",
                                lambda e_, yslot=yslot, fs=fs: e_.tensor_copy(
                                    yE_sb[yslot].ap()[:, (fs % 4) * TT: (fs % 4) * TT + 512],
                                    yE_ps.ap()[:, 0:512]),
                                waits=[ymm2], inc=sm_v)
                            yE_ev_s = P.emit("scalar",
                                lambda e_, yslot=yslot, fs=fs: e_.copy(
                                    yE_sb[yslot].ap()[:, (fs % 4) * TT + 512: (fs % 4 + 1) * TT],
                                    yE_ps.ap()[:, 512:1024]),
                                waits=[ymm2], inc=sm_s)
                            w2_issue()
                            if fs % 4 == 3 and stage >= 3.8:
                                fsb_ = fs // 4
                                last_stt = None
                                for ts in range(NTSH):
                                    gts = th * NTSH + ts
                                    half = ts % 2
                                    tw = [yE_ev_v, yE_ev_s, trhalf_tok[half]]
                                    for i in range(4):
                                        P.emit("tensor",
                                            lambda e_, yslot=yslot, i=i, ts=ts, half=half: e_.transpose(
                                                tr_ps2[half].ap()[:, i * 128:(i + 1) * 128],
                                                yE_sb[yslot].ap()[:, i * TT + ts * 128:
                                                                  i * TT + (ts + 1) * 128],
                                                idbf.ap()),
                                            waits=tw, inc=(sm_t if i == 3 else None))
                                        tw = []
                                    trk = sm_t.tok()
                                    if stage < 3.82:
                                        last_stt = P.emit("vector",
                                            lambda e_, half=half: e_.tensor_copy(
                                                gg_sb[0].ap()[:, 0:512],
                                                tr_ps2[half].ap()[:, 0:512]),
                                            waits=[trk], inc=sm_v)
                                        trhalf_tok[half] = last_stt
                                        continue
                                    wcol = (onescol_sb.ap() if e == 0
                                            else w_rt.ap()[:, gts * 12 + (e - 1): gts * 12 + e])
                                    last_stt = P.emit("vector",
                                        lambda e_, ts=ts, fsb_=fsb_, half=half, wcol=wcol:
                                        e_.scalar_tensor_tensor(
                                            T_acc.ap()[:, ts * D + fsb_ * 512:
                                                       ts * D + (fsb_ + 1) * 512],
                                            tr_ps2[half].ap()[:, 0:512],
                                            wcol,
                                            T_acc.ap()[:, ts * D + fsb_ * 512:
                                                       ts * D + (fsb_ + 1) * 512],
                                            ALU.mult, ALU.add),
                                        waits=[trk, tinit[ts]], inc=sm_v)
                                    trhalf_tok[half] = last_stt
                                yslot_free[0] = last_stt
                                yslot_free[1] = last_stt

                    for ts in range(NTSH if stage >= 3.9 else 0):
                        gts = th * NTSH + ts
                        P.emit("vector",
                            lambda e_, ts=ts: e_.tensor_tensor(
                                gg_sb[0].ap(), T_acc.ap()[:, ts * D:(ts + 1) * D],
                                gb_rep.ap(), ALU.mult))
                        P.drain("vector")
                        ftok = P.emit("vector",
                            lambda e_, gts=gts: e_.tensor_tensor(
                                x_tm.ap()[:, gts * D:(gts + 1) * D],
                                x_tm.ap()[:, gts * D:(gts + 1) * D],
                                gg_sb[0].ap(), ALU.add),
                            inc=sm_v)
                        final_tok_prev[ts] = ftok
                        P.emit("sync",
                            lambda e_, gts=gts: e_.dma_start(
                                out_d[gts * 128:(gts + 1) * 128, :],
                                x_tm.ap()[:, gts * D:(gts + 1) * D]),
                            waits=[ftok], inc=sm_st, inck=16)
                    gelu_tok = final_tok_prev[NTSH - 1]

        if stage >= 3.9:
            P.emit("sync", lambda e: None, waits=[(sm_st, 16 * NTS)])
        else:
            partial_store((sm_v, sm_v.n))
            for eng, sem in (("sync", sm_t), ("sync", sm_s)):
                P.emit(eng, lambda e: None, waits=[(sem, sem.n)])
        finish()
    return nc


_CACHE = {}


def _pair_permute_cols(w):
    shp = w.shape
    n2 = shp[-1]
    npr = n2 // 256
    w2 = w.reshape(shp[:-1] + (2, npr, 128))
    w3 = np.moveaxis(w2, -3, -2)
    return np.ascontiguousarray(w3.reshape(shp))


def kernel(L0_attn_x, L0_attn_W_dq, L0_attn_W_uq, L0_attn_W_dkv, L0_attn_W_uk,
           L0_attn_W_uv, L0_attn_W_o, L0_ffn_Ws1, L0_ffn_bs1, L0_ffn_Ws2,
           L0_ffn_bs2, L0_ffn_We1, L0_ffn_be1, L0_ffn_We2, L0_ffn_be2,
           L0_ffn_Wr, L0_ffn_br, L0_c, L0_wmod):
    f = np.float32
    asc = np.ascontiguousarray
    x = np.asarray(L0_attn_x, dtype=f)

    ws1p = _pair_permute_cols(np.asarray(L0_ffn_Ws1, f))
    we1p = _pair_permute_cols(np.asarray(L0_ffn_We1, f))
    bs1p = _pair_permute_cols(np.asarray(L0_ffn_bs1, f).reshape(1, -1))[0]
    be1p = _pair_permute_cols(np.asarray(L0_ffn_be1, f))

    mask = np.zeros((128, 128), f)
    mask[np.triu_indices(128, 1)] = -1e30

    common = {
        "w_dq": asc(np.asarray(L0_attn_W_dq, f)),
        "w_dqT": asc(np.asarray(L0_attn_W_dq, f).T),
        "w_uq": asc(np.asarray(L0_attn_W_uq, f)),
        "w_uqT": asc(np.asarray(L0_attn_W_uq, f).T),
        "w_dkvT": asc(np.asarray(L0_attn_W_dkv, f).T),
        "w_uk": asc(np.asarray(L0_attn_W_uk, f)),
        "w_uv": asc(np.asarray(L0_attn_W_uv, f)),
        "w_oT": asc(np.asarray(L0_attn_W_o, f).T),
        "wmod": asc(np.asarray(L0_wmod, f)),
        "ws1": ws1p,
        "ws2": asc(np.asarray(L0_ffn_Ws2, f)),
        "we1": we1p,
        "we2": asc(np.asarray(L0_ffn_We2, f)),
        "wr": asc(np.asarray(L0_ffn_Wr, f)),
        "br_rep": asc(np.broadcast_to(np.asarray(L0_ffn_br, f), (128, 12))),
        "bs1c": asc(bs1p.reshape(32, 128).T),
        "be1c": asc(be1p.reshape(NE, 48, 128).transpose(2, 0, 1)),
        "be2p": np.concatenate([np.asarray(L0_ffn_be2, f),
                                np.zeros((128 - NE, 1024), f)], axis=0),
        "bs2rp": asc(np.broadcast_to(np.asarray(L0_ffn_bs2, f), (128, 1024))),
        "ones1": np.ones((1, 128), f),
        "onescol": np.ones((128, 1), f),
        "ident32": np.eye(128, dtype=f),
        "epscol": np.full((128, 1), 1e-5, dtype=f),
        "identbf": np.eye(128, dtype=ml_dtypes.bfloat16),
        "maskbf": mask.astype(ml_dtypes.bfloat16),
    }
    c = np.asarray(L0_c, f)
    in_maps = []
    for b in range(8):
        m = dict(common)
        m["x"] = asc(x[b])
        m["c_cols"] = asc(c[b].reshape(8, 128).T)
        in_maps.append(m)

    import os as _os
    _stage = float(_os.environ.get("MK_STAGE", "4"))
    if _CACHE.get("stage") != _stage:
        _CACHE["nc"] = build_program(_stage)
        _CACHE["stage"] = _stage
    nc = _CACHE["nc"]
    res = run_bass_kernel_spmd(nc, in_maps, core_ids=list(range(8)))
    return np.stack([res.results[b]["out"] for b in range(8)], axis=0)


# version 03 (final)
# speedup vs baseline: 1.1029x; 1.0378x over previous
import numpy as np
import ml_dtypes
from contextlib import ExitStack

import concourse.bass as bass
import concourse.mybir as mybir
from concourse.bass_utils import run_bass_kernel_spmd

F32 = mybir.dt.float32
BF16 = mybir.dt.bfloat16
AF = mybir.ActivationFunctionType
ALU = mybir.AluOpType
AX = mybir.AxisListType

T = 2048
D = 1024
NTS = 16
NCH = 8
NH = 16
NE = 12
TAU = 1.5
TT = 1024
NTSH = 8
NG = 4
EPS = 1e-5


class Sem:
    def __init__(self, nc, ctx, name):
        self.h = ctx.enter_context(nc.semaphore(name))
        self.name = name
        self.n = 0

    def tok(self):
        return (self, self.n)


class Plan:
    ENGS = ("tensor", "vector", "scalar", "sync", "gpsimd")

    def __init__(self):
        self.ops = {e: [] for e in self.ENGS}
        self.seen = {e: {} for e in self.ENGS}

    def emit(self, eng, fn, waits=(), inc=None, inck=1):
        wl = []
        seen = self.seen[eng]
        for w in waits:
            if w is None:
                continue
            s, v = w
            if v <= 0 or seen.get(s.name, 0) >= v:
                continue
            assert v <= s.n, f"wait on future token {s.name}:{v}>{s.n}"
            seen[s.name] = v
            wl.append((s.h, v))
        if inc is not None:
            inc.n += inck

        def op(e, wl=wl, fn=fn, inc=inc, inck=inck):
            for h, v in wl:
                e.wait_ge(h, v)
            inst = fn(e)
            if inc is not None:
                inst.then_inc(inc.h, inck)
        self.ops[eng].append(op)
        return inc.tok() if inc is not None else None

    def drain(self, eng):
        self.ops[eng].append(lambda e: e.drain())


def ln_chain(P, x_tm, cols, scale_rep, shift_rep, h_stage, psums, id32,
             h_fm, hfm_tok, stat_v, stat_s, rep_tok, sm_t, sm_v, sm_s, epscol,
             rt_f32=None, wr_sb=None, logits_ps=None, upto=4):
    s1c, s2c, meanc, ex2c, nvarc, lnvc, rstdc = cols
    psA, psB = psums
    ps_free = [None, None]
    ps_free_s = [None, None]
    hst_free = [None, None]
    rtmm_tok = None
    for ts in range(NTS):
        par = ts % 2
        if upto <= 0:
            hfm_tok[ts] = stat_v[ts]
            continue
        P.emit("vector",
            lambda e, ts=ts: e.tensor_scalar(meanc.ap()[:, ts:ts + 1],
                                             s1c.ap()[:, ts:ts + 1], 1.0 / D, None, ALU.mult),
            waits=[stat_v[ts]])
        P.emit("vector",
            lambda e, ts=ts: e.tensor_scalar(ex2c.ap()[:, ts:ts + 1],
                                             s2c.ap()[:, ts:ts + 1], 1.0 / D, None, ALU.mult),
            waits=[stat_s[ts]])
        P.drain("vector")
        nv = P.emit("vector",
            lambda e, ts=ts: e.scalar_tensor_tensor(
                nvarc.ap()[:, ts:ts + 1], meanc.ap()[:, ts:ts + 1],
                meanc.ap()[:, ts:ts + 1], ex2c.ap()[:, ts:ts + 1],
                ALU.mult, ALU.subtract),
            inc=sm_v)
        P.emit("scalar",
            lambda e, ts=ts: e.activation(lnvc.ap()[:, ts:ts + 1], nvarc.ap()[:, ts:ts + 1],
                                          AF.Ln, bias=epscol.ap(), scale=-1.0),
            waits=[nv])
        P.drain("scalar")
        rs = P.emit("scalar",
            lambda e, ts=ts: e.activation(rstdc.ap()[:, ts:ts + 1], lnvc.ap()[:, ts:ts + 1],
                                          AF.Exp, scale=-0.5),
            inc=sm_s)
        if upto <= 1:
            hfm_tok[ts] = rs
            continue
        hw = [rs, hst_free[par], rep_tok]
        P.emit("vector",
            lambda e, ts=ts, par=par: e.tensor_scalar(
                h_stage[par].ap(), x_tm.ap()[:, ts * D:(ts + 1) * D],
                meanc.ap()[:, ts:ts + 1], rstdc.ap()[:, ts:ts + 1],
                ALU.subtract, ALU.mult),
            waits=hw)
        P.drain("vector")
        P.emit("vector",
            lambda e, par=par: e.tensor_tensor(h_stage[par].ap(), h_stage[par].ap(),
                                               scale_rep.ap(), ALU.mult))
        P.drain("vector")
        hs = P.emit("vector",
            lambda e, par=par: e.tensor_tensor(h_stage[par].ap(), h_stage[par].ap(),
                                               shift_rep.ap(), ALU.add),
            inc=sm_v)
        if upto <= 2:
            hfm_tok[ts] = hs
            continue
        last_ev = None
        for cb in range(2):
            ps = psA if cb == 0 else psB
            tw = [hs, ps_free[cb], ps_free_s[cb]]
            for i in range(4):
                c = cb * 4 + i
                P.emit("tensor",
                    lambda e, c=c, i=i, par=par, ps=ps: e.transpose(
                        ps.ap()[:, i * 128:(i + 1) * 128],
                        h_stage[par].ap()[:, c * 128:(c + 1) * 128],
                        id32.ap()),
                    waits=tw, inc=(sm_t if i == 3 else None))
                tw = []
            trk = sm_t.tok()
            if cb == 1:
                hst_free[par] = trk
            if upto <= 3:
                last_ev = trk
                ps_free[cb] = None
                hst_free[par] = None
                continue
            last_ev = P.emit("vector",
                lambda e, cb=cb, ts=ts, ps=ps: e.tensor_copy(
                    h_fm.ap().rearrange("p (c t j) -> p c t j", c=NCH, t=NTS)
                    [:, cb * 4:(cb + 1) * 4, ts:ts + 1, :],
                    ps.ap().rearrange("p (c j) -> p c j", c=4).unsqueeze(2)),
                waits=[trk], inc=sm_v)
            ps_free[cb] = last_ev
            if rt_f32 is not None:
                ps_free_s[cb] = P.emit("scalar",
                    lambda e, cb=cb, ps=ps: e.copy(
                        rt_f32.ap()[:, cb * 512:(cb + 1) * 512], ps.ap()),
                    waits=[trk, rtmm_tok, last_ev], inc=sm_s)
        hfm_tok[ts] = last_ev
        if rt_f32 is not None:
            for c in range(NCH):
                P.emit("tensor",
                    lambda e, c=c, ts=ts: e.matmul(
                        logits_ps.ap()[:, ts * 12:(ts + 1) * 12],
                        rt_f32.ap()[:, c * 128:(c + 1) * 128],
                        wr_sb.ap()[:, c * 12:(c + 1) * 12],
                        start=(ts == 0 and c == 0), stop=(ts == NTS - 1 and c == NCH - 1)),
                    waits=[ps_free_s[0], ps_free_s[1]],
                    inc=(sm_t if c == NCH - 1 else None))
            rtmm_tok = sm_t.tok()
    return rtmm_tok if rt_f32 is not None else hfm_tok[NTS - 1]


def build_program(stage=4.0, debug=False):
    nc = bass.Bass("TRN2", target_bir_lowering=False)
    P = Plan()

    di = {}
    def inp(name, shape, dt=F32):
        di[name] = nc.dram_tensor(name, list(shape), dt, kind="ExternalInput").ap()
        return di[name]

    x_d = inp("x", (T, D))
    inp("w_dq", (128, 1024)); inp("w_dqT", (1024, 128))
    inp("w_uq", (1024, 128)); inp("w_uqT", (128, 1024))
    inp("w_dkvT", (1024, 128)); inp("w_uk", (1024, 128))
    inp("w_uv", (1024, 128)); inp("w_oT", (1024, 1024))
    inp("wmod", (1024, 6144))
    inp("c_cols", (128, 8))
    inp("ws1", (1024, 4096))
    inp("ws2", (2048, 1024))
    inp("we1", (NE, 1024, 6144))
    inp("we2", (NE, 3072, 1024))
    inp("wr", (1024, 12))
    inp("br_rep", (128, 12))
    inp("bs1c", (128, 32))
    inp("be1c", (128, NE, 48))
    inp("be2p", (128, 1024))
    inp("bs2rp", (128, 1024))
    inp("ones1", (1, 128))
    inp("onescol", (128, 1))
    inp("epscol", (128, 1))
    inp("ident32", (128, 128))
    inp("identbf", (128, 128), BF16)
    inp("maskbf", (128, 128), BF16)
    out_d = nc.dram_tensor("out", [T, D], F32, kind="ExternalOutput").ap()

    with ExitStack() as top:
        def sballoc(ctx, name, shape, dt=F32):
            return ctx.enter_context(nc.sbuf_tensor(name, list(shape), dt))
        sb = lambda name, shape, dt=F32: sballoc(top, name, shape, dt)

        x_tm = sb("x_tm", (128, NTS * D))
        h_fm = sb("h_fm", (128, NCH * T), BF16)
        gb_rep = sb("gb_rep", (128, D), BF16)
        wr_sb = sb("wr_sb", (128, NCH * 12))
        br_sb = sb("br_sb", (128, 12))
        bs1c_sb = sb("bs1c_sb", (128, 32))
        id32 = sb("id32", (128, 128))
        idbf = sb("idbf", (128, 128), BF16)
        mask_sb = sb("mask_sb", (128, 128), BF16)
        ones1_sb = sb("ones1_sb", (1, 128))
        onescol_sb = sb("onescol_sb", (128, 1))
        epscol_sb = sb("epscol_sb", (128, 1))
        c_cols_sb = sb("c_cols_sb", (128, 8))
        silu_c = sb("silu_c", (128, 8))
        s1c = sb("s1c", (128, NTS)); s2c = sb("s2c", (128, NTS))
        meanc = sb("meanc", (128, NTS)); ex2c = sb("ex2c", (128, NTS))
        nvarc = sb("nvarc", (128, NTS)); lnvc = sb("lnvc", (128, NTS))
        rstdc = sb("rstdc", (128, NTS))
        sq_dump = sb("sq_dump", (128, D), BF16)
        rl = sb("rl", (128, 12)); rm1 = sb("rm1", (128, 1)); rm1n = sb("rm1n", (128, 1))
        req = sb("req", (128, 12)); rl2 = sb("rl2", (128, 12)); rm2 = sb("rm2", (128, 1))
        rex = sb("rex", (128, 12)); rmk = sb("rmk", (128, 12)); rp = sb("rp", (128, 12))
        rZs = sb("rZs", (128, 1)); rrZ = sb("rrZ", (128, 1))

        sm_t = Sem(nc, top, "sm_t")
        sm_v = Sem(nc, top, "sm_v")
        sm_s = Sem(nc, top, "sm_s")
        sm_px = Sem(nc, top, "sm_px")
        sm_st = Sem(nc, top, "sm_st")
        sm_ga = Sem(nc, top, "sm_ga")
        sm_w1 = [Sem(nc, top, "sm_w1a"), Sem(nc, top, "sm_w1b")]
        sm_w2 = [Sem(nc, top, "sm_w2a"), Sem(nc, top, "sm_w2b")]
        sm_wm = [Sem(nc, top, "sm_wma"), Sem(nc, top, "sm_wmb")]
        sm_gb = Sem(nc, top, "sm_gb")
        sm_gc = Sem(nc, top, "sm_gc")
        sm_wo = Sem(nc, top, "sm_wo")

        cols = (s1c, s2c, meanc, ex2c, nvarc, lnvc, rstdc)

        late = top.enter_context(ExitStack())
        w_rt = sballoc(late, "w_rt", (128, NTS * 12))
        wT = sballoc(late, "wTq", (12, T), BF16)
        be2s_bf = sballoc(late, "be2s", (128, D), BF16)
        be1c_sb = sballoc(late, "be1c_sb", (128, NE * 48), BF16)

        mid = ExitStack()
        k_eff = sballoc(mid, "k_eff", (128, 8 * 128), BF16)
        v_eff = sballoc(mid, "v_eff", (128, D), BF16)
        ca1_rep = sballoc(mid, "ca1_rep", (128, D)); sa_rep = sballoc(mid, "sa_rep", (128, D))
        cb1_rep = sballoc(mid, "cb1_rep", (128, D)); sb_rep = sballoc(mid, "sb_rep", (128, D))

        def ld(dst_ap, src_ap, waits=()):
            return P.emit("sync", lambda e, d=dst_ap, s=src_ap: e.dma_start(d, s),
                          waits=waits, inc=sm_px, inck=16)

        ld(id32.ap(), di["ident32"]); ld(idbf.ap(), di["identbf"])
        ld(mask_sb.ap(), di["maskbf"]); ld(ones1_sb.ap(), di["ones1"])
        ld(onescol_sb.ap(), di["onescol"]); ld(c_cols_sb.ap(), di["c_cols"])
        ld(epscol_sb.ap(), di["epscol"])
        ld(br_sb.ap(), di["br_rep"]); ld(bs1c_sb.ap(), di["bs1c"])
        for kc in range(NCH):
            ld(wr_sb.ap()[:, kc * 12:(kc + 1) * 12], di["wr"][kc * 128:(kc + 1) * 128, :])
        for ts in range(NTS):
            ld(x_tm.ap()[:, ts * D:(ts + 1) * D], x_d[ts * 128:(ts + 1) * 128, :])
        px_all = sm_px.tok()
        consts_tok = px_all
        x_tok = [px_all] * NTS

        def gld(dst_ap, src_ap, sem, waits=()):
            return P.emit("gpsimd", lambda e, d=dst_ap, s=src_ap: e.dma_start(d, s),
                          waits=waits, inc=sem, inck=16)

        def chunked(dst_sb, src_ap, nk):
            return gld(dst_sb.ap().rearrange("p (k c) -> p k c", k=nk),
                       src_ap.rearrange("(k p) c -> p k c", p=128), sm_ga)

        be1c_tok = gld(be1c_sb.ap().rearrange("p (e s) -> p e s", e=NE), di["be1c"], sm_gb)

        hfm_tok = [None] * NTS
        with ExitStack() as pre:
            psb = lambda name, shape, dt=F32: sballoc(pre, name, shape, dt)
            w_dq_sb = psb("w_dq_sb", (128, 1024), BF16)
            w_uq_sb = psb("w_uq_sb", (128, 8 * 128), BF16)
            w_uk_sb = psb("w_uk_sb", (128, 8 * 128), BF16)
            w_uv_sb = psb("w_uv_sb", (128, 8 * 128), BF16)
            w_oT_sb = psb("w_oT_sb", (128, 8 * 512), BF16)
            mod_row = psb("mod_row", (1, 6144))
            mrow_g1 = psb("mrow_g1", (1, 1024)); mrow_g2 = psb("mrow_g2", (1, 1024))
            wmodbuf = [psb(f"wmodb{i}", (128, 8 * 256), F32) for i in range(2)]
            ga_rep = psb("ga_rep", (128, D), BF16)
            h_stage = [psb(f"h_stage{i}", (128, D)) for i in range(2)]
            B1_bf = psb("B1_bf", (128, 128), BF16)

            gld(w_dq_sb.ap(), di["w_dq"], sm_ga)
            chunked(w_uq_sb, di["w_uq"], 8)
            chunked(w_uk_sb, di["w_uk"], 8)
            chunked(w_uv_sb, di["w_uv"], 8)
            attw_tok = sm_ga.tok()
            woT_tok0 = gld(w_oT_sb.ap().rearrange("p (k c) -> p k c", k=8),
                           di["w_oT"][:, 0:512].rearrange("(k p) c -> p k c", p=128),
                           sm_wo)

            silu_tok = P.emit("scalar",
                              lambda e: e.activation(silu_c.ap(), c_cols_sb.ap(), AF.Silu),
                              waits=[consts_tok], inc=sm_s)

            stat1_v = []; stat1_s = []
            for ts in range(NTS):
                xs = x_tm.ap()[:, ts * D:(ts + 1) * D]
                stat1_v.append(P.emit("vector",
                    lambda e, xs=xs, ts=ts: e.reduce_sum(s1c.ap()[:, ts:ts + 1], xs, axis=AX.X),
                    waits=[x_tok[ts]], inc=sm_v))
                stat1_s.append(P.emit("scalar",
                    lambda e, xs=xs, ts=ts: e.activation(sq_dump.ap(), xs, AF.Square,
                                                         accum_out=s2c.ap()[:, ts:ts + 1]),
                    waits=[x_tok[ts]], inc=sm_s))

            with nc.psum_tensor("mod_ps", [1, 256], F32) as mod_ps, \
                 nc.psum_tensor("pp1", [128, 512], F32) as pp1, \
                 nc.psum_tensor("pp2", [128, 512], F32) as pp2:

                NMB = 24 if stage >= 0.5 else 0
                mod_evac = [None] * max(NMB, 1)
                wm_cnt = [0, 0]
                wm_tok = {}

                def wmod_dma(j, waits):
                    sl = j % 2
                    wm_cnt[sl] += 1
                    src = di["wmod"][:, j * 256:(j + 1) * 256].rearrange("(k p) c -> p k c", p=128)
                    gld(wmodbuf[sl].ap().rearrange("p (k c) -> p k c", k=8), src,
                        sm_wm[sl], waits=waits)
                    wm_tok[j] = (sm_wm[sl], 16 * wm_cnt[sl])

                wmod_dma(0, [silu_tok])
                wmod_dma(1, [])
                for j in range(NMB):
                    sl = j % 2
                    for kc in range(NCH):
                        P.emit("tensor",
                            lambda e, kc=kc, sl=sl: e.matmul(
                                mod_ps.ap(),
                                silu_c.ap()[:, kc:kc + 1],
                                wmodbuf[sl].ap()[:, kc * 256:(kc + 1) * 256],
                                start=(kc == 0), stop=(kc == NCH - 1)),
                            waits=[wm_tok[j], silu_tok, mod_evac[j - 1] if j >= 1 else None],
                            inc=(sm_t if kc == NCH - 1 else None))
                    mm_tok = sm_t.tok()
                    mod_evac[j] = P.emit("vector",
                        lambda e, j=j: e.tensor_copy(mod_row.ap()[:, j * 256:(j + 1) * 256],
                                                     mod_ps.ap()),
                        waits=[mm_tok], inc=sm_v)
                    if j + 2 < NMB:
                        wmod_dma(j + 2, [mm_tok])
                mod_tok = mod_evac[NMB - 1] if NMB else None

                def vop(fn, waits=(), dr=False):
                    t = P.emit("vector", fn, waits=waits, inc=sm_v)
                    if dr:
                        P.drain("vector")
                    return t
                if stage < 0.6:
                    modp_tok = None
                    reps_tok = None
                    b1_tok = None
                    keff_tok = None
                    veff_tok = None
                if stage >= 0.6: vop(lambda e: e.tensor_scalar(mod_row.ap()[:, 1024:2048],
                                              mod_row.ap()[:, 1024:2048], 1.0, None, ALU.add),
                    waits=[mod_tok])
                vop(lambda e: e.tensor_scalar(mod_row.ap()[:, 4096:5120],
                                              mod_row.ap()[:, 4096:5120], 1.0, None, ALU.add),
                    dr=True)
                for off in ((2048, 5120) if stage >= 0.6 else ()):
                    g = mod_row.ap()[:, off:off + 1024]
                    vop(lambda e, g=g: e.tensor_scalar(mrow_g1.ap(), g, -1.0, None, ALU.mult), dr=True)
                    vop(lambda e, g=g: e.tensor_tensor(mrow_g1.ap(), g, mrow_g1.ap(), ALU.max), dr=True)
                    vop(lambda e: e.tensor_scalar(mrow_g1.ap(), mrow_g1.ap(), 0.25, None, ALU.max))
                    vop(lambda e, g=g: e.tensor_scalar(mrow_g2.ap(), g, 0.0, None, ALU.is_ge), dr=True)
                    vop(lambda e: e.tensor_scalar(mrow_g2.ap(), mrow_g2.ap(), 2.0, -1.0,
                                                  ALU.mult, ALU.add), dr=True)
                    vop(lambda e, g=g: e.tensor_tensor(g, mrow_g1.ap(), mrow_g2.ap(), ALU.mult),
                        dr=True)
                if stage >= 0.6:
                    modp_tok = sm_v.tok()

                reps = ([(sa_rep, 0), (ca1_rep, 1024), (ga_rep, 2048),
                        (sb_rep, 3072), (cb1_rep, 4096), (gb_rep, 5120)]
                        if stage >= 0.6 else [])
                prev_rep_ev = None
                for dst, off in reps:
                    for n2 in range(2):
                        mm = P.emit("tensor",
                            lambda e, off=off, n2=n2: e.matmul(
                                pp1.ap(), ones1_sb.ap(),
                                mod_row.ap()[:, off + n2 * 512: off + (n2 + 1) * 512],
                                start=True, stop=True),
                            waits=[modp_tok, consts_tok, prev_rep_ev], inc=sm_t)
                        prev_rep_ev = P.emit("vector",
                            lambda e, dst=dst, n2=n2: e.tensor_copy(
                                dst.ap()[:, n2 * 512:(n2 + 1) * 512], pp1.ap()),
                            waits=[mm], inc=sm_v)
                if stage >= 0.6:
                    reps_tok = prev_rep_ev

                for kc in (range(NCH) if stage >= 0.8 else ()):
                    P.emit("tensor",
                        lambda e, kc=kc: e.matmul(
                            pp2.ap()[:, 0:128],
                            w_uq_sb.ap()[:, kc * 128:(kc + 1) * 128],
                            w_uk_sb.ap()[:, kc * 128:(kc + 1) * 128],
                            start=(kc == 0), stop=(kc == NCH - 1)),
                        waits=[attw_tok], inc=(sm_t if kc == NCH - 1 else None))
                if stage >= 0.8:
                    b1_mm = sm_t.tok()
                    b1_tok = P.emit("vector",
                        lambda e: e.tensor_copy(B1_bf.ap(), pp2.ap()[:, 0:128]),
                        waits=[b1_mm], inc=sm_v)
                keff_ev = None
                for s in (range(NCH) if stage >= 0.8 else ()):
                    mm = P.emit("tensor",
                        lambda e, s=s: e.matmul(
                            pp2.ap()[:, 128:256],
                            w_dq_sb.ap()[:, s * 128:(s + 1) * 128],
                            B1_bf.ap(), start=True, stop=True),
                        waits=[b1_tok, keff_ev], inc=sm_t)
                    keff_ev = P.emit("vector",
                        lambda e, s=s: e.tensor_copy(k_eff.ap()[:, s * 128:(s + 1) * 128],
                                                     pp2.ap()[:, 128:256]),
                        waits=[mm], inc=sm_v)
                if stage >= 0.8:
                    keff_tok = keff_ev

                veff_ev = prev_rep_ev
                woT_tok = [woT_tok0, None]
                for n2 in (range(2) if stage >= 0.8 else ()):
                    for kc in range(NCH):
                        P.emit("tensor",
                            lambda e, kc=kc: e.matmul(
                                pp1.ap(),
                                w_uv_sb.ap()[:, kc * 128:(kc + 1) * 128],
                                w_oT_sb.ap()[:, kc * 512:(kc + 1) * 512],
                                start=(kc == 0), stop=(kc == NCH - 1)),
                            waits=[attw_tok, woT_tok[n2], reps_tok, veff_ev],
                            inc=(sm_t if kc == NCH - 1 else None))
                    mm = sm_t.tok()
                    if n2 == 0:
                        woT_tok[1] = gld(
                            w_oT_sb.ap().rearrange("p (k c) -> p k c", k=8),
                            di["w_oT"][:, 512:1024].rearrange("(k p) c -> p k c", p=128),
                            sm_wo, waits=[mm])
                    veff_ev = P.emit("vector",
                        lambda e, n2=n2: e.tensor_tensor(
                            v_eff.ap()[:, n2 * 512:(n2 + 1) * 512], pp1.ap(),
                            ga_rep.ap()[:, n2 * 512:(n2 + 1) * 512], ALU.mult),
                        waits=[mm], inc=sm_v)
                if stage >= 0.8:
                    veff_tok = veff_ev

            with nc.psum_tensor("lnpA", [128, 512], F32) as lnpA, \
                 nc.psum_tensor("lnpB", [128, 512], F32) as lnpB:
                if stage >= 0.9:
                    ln_chain(P, x_tm, cols, ca1_rep, sa_rep, h_stage, (lnpA, lnpB), id32,
                             h_fm, hfm_tok, stat1_v, stat1_s, reps_tok, sm_t, sm_v, sm_s,
                             epscol_sb)

        def partial_store(toks):
            for ts in range(NTS):
                P.emit("sync",
                    lambda e_, ts=ts: e_.dma_start(
                        out_d[ts * 128:(ts + 1) * 128, :],
                        x_tm.ap()[:, ts * D:(ts + 1) * D]),
                    waits=[toks[ts] if isinstance(toks, list) else toks,
                           hfm_tok[NTS - 1]] if hfm_tok[NTS - 1] else
                          [toks[ts] if isinstance(toks, list) else toks],
                    inc=sm_st, inck=16)
            P.emit("sync", lambda e: None, waits=[(sm_st, sm_st.n)])

        def finish():
            with nc.Block() as blk:
                blk.tensor(lambda e: [op(e) for op in P.ops["tensor"]])
                blk.vector(lambda e: [op(e) for op in P.ops["vector"]])
                blk.scalar(lambda e: [op(e) for op in P.ops["scalar"]])
                blk.sync(lambda e: [op(e) for op in P.ops["sync"]])
                blk.gpsimd(lambda e: [op(e) for op in P.ops["gpsimd"]])

        if stage <= 1.0:
            partial_store(x_tok)
            mid.close()
            finish()
            return nc

        xadd_tok = [None] * NTS
        with ExitStack() as att:
            asb = lambda name, shape, dt=BF16: sballoc(att, name, shape, dt)
            w_dqT_sb = asb("w_dqT_sb", (128, 8 * 128))
            w_uqT_sb = asb("w_uqT_sb", (128, 1024))
            w_dkvT_sb = asb("w_dkvT_sb", (128, 8 * 128))
            q_fm = asb("q_fm", (128, NCH * T))
            ckv_fm = asb("ckv_fm", (128, T))
            ckv_tm = asb("ckv_tm", (128, NTS * 128))
            gld(w_dqT_sb.ap().rearrange("p (k c) -> p k c", k=8),
                di["w_dqT"].rearrange("(k p) c -> p k c", p=128), sm_gc,
                waits=[veff_tok])
            gld(w_uqT_sb.ap(), di["w_uqT"], sm_gc)
            gld(w_dkvT_sb.ap().rearrange("p (k c) -> p k c", k=8),
                di["w_dkvT"].rearrange("(k p) c -> p k c", p=128), sm_gc)
            attw2_tok = sm_gc.tok()

            with ExitStack() as apre:
                q1_sb = sballoc(apre, "q1_sb", (128, T), BF16)
                with nc.psum_tensor("ap1", [128, 512], F32) as ap1, \
                     nc.psum_tensor("ap2", [128, 512], F32) as ap2:
                    ev1 = None
                    for tg in range(4):
                        for kc in range(NCH):
                            P.emit("tensor",
                                lambda e, kc=kc, tg=tg: e.matmul(
                                    ap1.ap(),
                                    w_dkvT_sb.ap()[:, kc * 128:(kc + 1) * 128],
                                    h_fm.ap()[:, kc * T + tg * 512: kc * T + (tg + 1) * 512],
                                    start=(kc == 0), stop=(kc == NCH - 1)),
                                waits=[attw2_tok, hfm_tok[4 * tg + 3], ev1],
                                inc=(sm_t if kc == NCH - 1 else None))
                        mm = sm_t.tok()
                        ev1 = P.emit("vector",
                            lambda e, tg=tg: e.tensor_copy(
                                ckv_fm.ap()[:, tg * 512:(tg + 1) * 512], ap1.ap()),
                            waits=[mm], inc=sm_v)
                    ev2 = None
                    for tg in range(4):
                        for kc in range(NCH):
                            P.emit("tensor",
                                lambda e, kc=kc, tg=tg: e.matmul(
                                    ap2.ap(),
                                    w_dqT_sb.ap()[:, kc * 128:(kc + 1) * 128],
                                    h_fm.ap()[:, kc * T + tg * 512: kc * T + (tg + 1) * 512],
                                    start=(kc == 0), stop=(kc == NCH - 1)),
                                waits=[ev2], inc=(sm_t if kc == NCH - 1 else None))
                        mm = sm_t.tok()
                        ev2 = P.emit("vector",
                            lambda e, tg=tg: e.tensor_copy(
                                q1_sb.ap()[:, tg * 512:(tg + 1) * 512], ap2.ap()),
                            waits=[mm], inc=sm_v)
                    q1_tok = ev2
                    ev3 = None
                    for ts in range(NTS):
                        for kc in range(NCH):
                            P.emit("tensor",
                                lambda e, kc=kc, ts=ts: e.matmul(
                                    ap1.ap()[:, 0:128],
                                    h_fm.ap()[:, kc * T + ts * 128: kc * T + (ts + 1) * 128],
                                    w_dkvT_sb.ap()[:, kc * 128:(kc + 1) * 128],
                                    start=(kc == 0), stop=(kc == NCH - 1)),
                                waits=[ev3], inc=(sm_t if kc == NCH - 1 else None))
                        mm = sm_t.tok()
                        ev3 = P.emit("vector",
                            lambda e, ts=ts: e.tensor_copy(
                                ckv_tm.ap()[:, ts * 128:(ts + 1) * 128], ap1.ap()[:, 0:128]),
                            waits=[mm], inc=sm_v)
                    ckvtm_tok = ev3
                    ev4 = None
                    for s in range(NCH):
                        for tg in range(4):
                            mm = P.emit("tensor",
                                lambda e, s=s, tg=tg: e.matmul(
                                    ap2.ap(),
                                    w_uqT_sb.ap()[:, s * 128:(s + 1) * 128],
                                    q1_sb.ap()[:, tg * 512:(tg + 1) * 512],
                                    start=True, stop=True),
                                waits=[q1_tok, ev4], inc=sm_t)
                            ev4 = P.emit("vector",
                                lambda e, s=s, tg=tg: e.tensor_copy(
                                    q_fm.ap()[:, s * T + tg * 512: s * T + (tg + 1) * 512],
                                    ap2.ap()),
                                waits=[mm], inc=sm_v)
                    qfm_tok = ev4

            ql_bf2 = [asb("ql_bfA", (128, 512)), asb("ql_bfB", (128, 512))]
            P_bf = [asb(f"P_bf{i}", (128, T)) for i in range(2)]
            PT_bf = asb("PT_bf", (128, 16 * 128))
            ctx_bf = [asb(f"ctx_bf{i}", (128, 128)) for i in range(2)]
            y_fm = asb("y_fm", (128, 8 * 512))
            m1col = sballoc(att, "m1col", (128, 8))
            lcol = sballoc(att, "lcol", (128, 8))
            rcol = sballoc(att, "rcol", (128, 8))

            with nc.psum_tensor("S_ps", [128, 2048], F32) as S_ps, \
                 nc.psum_tensor("ql_ps", [128, 512], F32) as ql_ps, \
                 nc.psum_tensor("PT_ps", [128, 1024], BF16) as PT_ps, \
                 nc.psum_tensor("ctx_ps", [128, 512], F32) as ctx_ps, \
                 nc.psum_tensor("y_ps", [128, 512], F32) as y_ps:

                st_ = {
                    "exp": [None, None],
                    "ptd": [None, None],
                    "qlev": [None, None],
                    "ctxev": [None, None],
                    "ptps": None,
                    "yfmev": None,
                    "ymm": None,
                    "smm": {}, "mx": {}, "pn": {},
                }

                def ph_ql(h, g, buf):
                    hp = h // 2
                    po = 64 * (h % 2)
                    mm = P.emit("tensor",
                        lambda e, hp=hp, po=po, g=g: e.matmul(
                            ql_ps.ap(),
                            k_eff.ap()[po:po + 64, hp * 128:(hp + 1) * 128],
                            q_fm.ap()[po:po + 64, hp * T + g * 512: hp * T + (g + 1) * 512],
                            start=True, stop=True),
                        waits=[qfm_tok, keff_tok, st_["qlev"][0], st_["qlev"][1]],
                        inc=sm_t)
                    st_["qlev"][buf] = P.emit("vector",
                        lambda e, buf=buf: e.tensor_scalar(
                            ql_bf2[buf].ap(), ql_ps.ap(), 0.125, None, ALU.mult),
                        waits=[mm], inc=sm_v)

                def ph_S(h, qt, buf, sbase, exw):
                    qtl = qt % 4
                    kmax = 128 * (qt + 1)
                    s_waits = [st_["qlev"][buf]] + exw
                    for b in range((kmax + 511) // 512):
                        lo = b * 512
                        hi = min(kmax, lo + 512)
                        lastb = (hi == kmax)
                        P.emit("tensor",
                            lambda e, lo=lo, hi=hi, qtl=qtl, buf=buf, sbase=sbase: e.matmul(
                                S_ps.ap()[:, sbase + lo: sbase + hi],
                                ql_bf2[buf].ap()[:, qtl * 128:(qtl + 1) * 128],
                                ckv_fm.ap()[:, lo:hi],
                                start=True, stop=(not lastb)),
                            waits=s_waits, inc=None)
                        s_waits = []
                    st_["smm"][(h, qt)] = P.emit("tensor",
                        lambda e, qt=qt, kmax=kmax, sbase=sbase: e.matmul(
                            S_ps.ap()[:, sbase + qt * 128: sbase + kmax],
                            idbf.ap(), mask_sb.ap(),
                            start=False, stop=True),
                        waits=[], inc=sm_t)

                def ph_max(h, qt, sbase, col):
                    kmax = 128 * (qt + 1)
                    st_["mx"][(h, qt)] = P.emit("vector",
                        lambda e, kmax=kmax, col=col, sbase=sbase: e.reduce_max(
                            m1col.ap()[:, col:col + 1],
                            S_ps.ap()[:, sbase: sbase + kmax], axis=AX.X, negate=True),
                        waits=[st_["smm"][(h, qt)]], inc=sm_v)

                def ph_exp(h, qt, buf, sbase, col):
                    kmax = 128 * (qt + 1)
                    st_["exp"][buf] = P.emit("scalar",
                        lambda e, kmax=kmax, buf=buf, col=col, sbase=sbase: e.activation(
                            P_bf[buf].ap()[:, 0:kmax],
                            S_ps.ap()[:, sbase: sbase + kmax], AF.Exp,
                            bias=m1col.ap()[:, col:col + 1], scale=1.0,
                            accum_out=lcol.ap()[:, col:col + 1]),
                        waits=[st_["mx"][(h, qt)], st_["ptd"][buf]], inc=sm_s)

                def ph_pn(h, qt, buf, col):
                    kmax = 128 * (qt + 1)
                    P.emit("vector",
                        lambda e, col=col: e.reciprocal(
                            rcol.ap()[:, col:col + 1], lcol.ap()[:, col:col + 1]),
                        waits=[st_["exp"][buf]])
                    P.drain("vector")
                    st_["pn"][(h, qt)] = P.emit("vector",
                        lambda e, kmax=kmax, buf=buf, col=col: e.tensor_scalar(
                            P_bf[buf].ap()[:, 0:kmax], P_bf[buf].ap()[:, 0:kmax],
                            rcol.ap()[:, col:col + 1], None, ALU.mult),
                        inc=sm_v)

                def ph_ptcy(h, qt, buf, ptbase, first_y):
                    qtl = qt % 4
                    po = 64 * (h % 2)
                    nsc = qt + 1
                    first_ctx = True
                    for scb in range(0, nsc, 8):
                        scs = list(range(scb, min(nsc, scb + 8)))
                        tw = [st_["pn"][(h, qt)], st_["ptps"]]
                        for i, sc in enumerate(scs):
                            P.emit("tensor",
                                lambda e, sc=sc, i=i, buf=buf: e.transpose(
                                    PT_ps.ap()[:, i * 128:(i + 1) * 128],
                                    P_bf[buf].ap()[:, sc * 128:(sc + 1) * 128],
                                    idbf.ap()),
                                waits=tw, inc=(sm_t if i == len(scs) - 1 else None))
                            tw = []
                        trk = sm_t.tok()
                        ptb_evac = P.emit("scalar",
                            lambda e, n=len(scs), scb=scb, ptbase=ptbase: e.copy(
                                PT_bf.ap()[:, ptbase + scb * 128: ptbase + (scb + n) * 128],
                                PT_ps.ap()[:, 0:n * 128]),
                            waits=[trk], inc=sm_s)
                        st_["ptps"] = ptb_evac
                        for sc in scs:
                            cw = [ptb_evac]
                            if first_ctx:
                                cw += [ckvtm_tok, st_["ctxev"][0], st_["ctxev"][1]]
                                first_ctx = False
                            P.emit("tensor",
                                lambda e, sc=sc, nsc=nsc, ptbase=ptbase: e.matmul(
                                    ctx_ps.ap()[:, 0:128],
                                    ckv_tm.ap()[:, sc * 128:(sc + 1) * 128],
                                    PT_bf.ap()[:, ptbase + sc * 128: ptbase + (sc + 1) * 128],
                                    start=(sc == 0), stop=(sc == nsc - 1)),
                                waits=cw, inc=(sm_t if sc == nsc - 1 else None))
                    st_["ptd"][buf] = sm_t.tok()
                    ctx_mm = sm_t.tok()
                    st_["ctxev"][buf] = P.emit("vector",
                        lambda e, buf=buf: e.tensor_copy(ctx_bf[buf].ap(),
                                                         ctx_ps.ap()[:, 0:128]),
                        waits=[ctx_mm], inc=sm_v)
                    st_["ymm"] = P.emit("tensor",
                        lambda e, po=po, qtl=qtl, h=h, buf=buf: e.matmul(
                            y_ps.ap()[po:po + 64, qtl * 128:(qtl + 1) * 128],
                            v_eff.ap()[:, h * 64:(h + 1) * 64],
                            ctx_bf[buf].ap(),
                            start=True, stop=True),
                        waits=[st_["ctxev"][buf], veff_tok,
                               st_["yfmev"] if first_y else None],
                        inc=sm_t)

                for g in range(NG):
                    if g < 2:
                        for hp in range(8):
                            h0, h1 = 2 * hp, 2 * hp + 1
                            ph_ql(h0, g, 0)
                            ph_ql(h1, g, 1)
                            for qtl in range(4):
                                qt = 4 * g + qtl
                                ph_S(h0, qt, 0, 0, [st_["exp"][0]])
                                ph_S(h1, qt, 1, 1024, [st_["exp"][1]])
                                if qtl > 0:
                                    ph_ptcy(h0, qt - 1, 0, 0, qtl == 1)
                                    ph_ptcy(h1, qt - 1, 1, 1024, False)
                                ph_max(h0, qt, 0, qtl)
                                ph_max(h1, qt, 1024, 4 + qtl)
                                ph_exp(h0, qt, 0, 0, qtl)
                                ph_exp(h1, qt, 1, 1024, 4 + qtl)
                                ph_pn(h0, qt, 0, qtl)
                                ph_pn(h1, qt, 1, 4 + qtl)
                            ph_ptcy(h0, 4 * g + 3, 0, 0, False)
                            ph_ptcy(h1, 4 * g + 3, 1, 1024, False)
                            st_["yfmev"] = P.emit("vector",
                                lambda e, hp=hp: e.tensor_copy(
                                    y_fm.ap()[:, hp * 512:(hp + 1) * 512], y_ps.ap()),
                                waits=[st_["ymm"]], inc=sm_v)
                    else:
                        for h in range(NH):
                            hp = h // 2
                            buf_prev = None
                            ph_ql(h, g, h % 2)
                            for qtl in range(4):
                                qt = 4 * g + qtl
                                buf = qtl % 2
                                ph_S(h, qt, h % 2, 0, [st_["exp"][0], st_["exp"][1]])
                                if qtl > 0:
                                    ph_ptcy(h, qt - 1, (qtl - 1) % 2, 0,
                                            (h % 2 == 0 and qtl == 1))
                                ph_max(h, qt, 0, qtl)
                                ph_exp(h, qt, buf, 0, qtl)
                                ph_pn(h, qt, buf, qtl)
                            ph_ptcy(h, 4 * g + 3, 1, 0, False)
                            if h % 2 == 1:
                                st_["yfmev"] = P.emit("vector",
                                    lambda e, hp=hp: e.tensor_copy(
                                        y_fm.ap()[:, hp * 512:(hp + 1) * 512], y_ps.ap()),
                                    waits=[st_["ymm"]], inc=sm_v)
                    for j in range(4):
                        ts = 4 * g + j
                        prev_add = None
                        for cb in range(2):
                            tw = [st_["yfmev"], st_["ptps"], prev_add]
                            for i in range(4):
                                c = cb * 4 + i
                                P.emit("tensor",
                                    lambda e, c=c, i=i, j=j: e.transpose(
                                        PT_ps.ap()[:, i * 128:(i + 1) * 128],
                                        y_fm.ap()[:, c * 512 + j * 128: c * 512 + (j + 1) * 128],
                                        idbf.ap()),
                                    waits=tw, inc=(sm_t if i == 3 else None))
                                tw = []
                            trk = sm_t.tok()
                            prev_add = P.emit("vector",
                                lambda e, ts=ts, cb=cb: e.tensor_tensor(
                                    x_tm.ap()[:, ts * D + cb * 512: ts * D + (cb + 1) * 512],
                                    x_tm.ap()[:, ts * D + cb * 512: ts * D + (cb + 1) * 512],
                                    PT_ps.ap()[:, 0:512], ALU.add),
                                waits=[trk], inc=sm_v)
                            st_["ptps"] = prev_add
                        xadd_tok[ts] = prev_add

                if debug:
                    allw = [xadd_tok[NTS - 1], (sm_v, sm_v.n), (sm_s, sm_s.n),
                            (sm_t, sm_t.n)]
                    def dbg_store(name, sbuf, shape, dt):
                        dd = nc.dram_tensor("dbg_" + name, list(shape), dt,
                                            kind="ExternalOutput").ap()
                        P.emit("sync",
                               lambda e, dd=dd, a=sbuf.ap(): e.dma_start(dd, a),
                               waits=allw, inc=sm_st, inck=16)
                    dbg_store("h_fm", h_fm, (128, NCH * T), BF16)
                    dbg_store("ckv_fm", ckv_fm, (128, T), BF16)
                    dbg_store("ckv_tm", ckv_tm, (128, NTS * 128), BF16)
                    dbg_store("q_fm", q_fm, (128, NCH * T), BF16)
                    dbg_store("k_eff", k_eff, (128, 8 * 128), BF16)
                    dbg_store("v_eff", v_eff, (128, D), BF16)
                    dbg_store("P_bf", P_bf[1], (128, T), BF16)
                    dbg_store("PT_bf", PT_bf, (128, 16 * 128), BF16)
                    dbg_store("y_fm", y_fm, (128, 8 * 512), BF16)
                    dbg_store("ca1", ca1_rep, (128, D), F32)
                    dbg_store("sa", sa_rep, (128, D), F32)

        if stage <= 2:
            partial_store(xadd_tok)
            mid.close()
            finish()
            return nc

        hfm2_tok = [None] * NTS
        with ExitStack() as ln2s:
            rt_f32 = sballoc(ln2s, "rt_f32", (128, D))
            h_stage2 = [sballoc(ln2s, f"h2st{i}", (128, D)) for i in range(2)]
            be2_f = sballoc(ln2s, "be2_f", (128, 1024))
            bs2r_f = sballoc(ln2s, "bs2r_f", (128, 1024))
            if stage >= 2.508:
                t1 = ld(be2_f.ap(), di["be2p"], waits=[xadd_tok[NTS - 1]])
                t2 = ld(bs2r_f.ap(), di["bs2rp"])
                if stage >= 2.5085 or stage >= 2.52:
                    be2s_tok = P.emit("vector",
                        lambda e: e.tensor_tensor(be2s_bf.ap(), be2_f.ap(), bs2r_f.ap(), ALU.add),
                        waits=[t1, t2], inc=sm_v)
                else:
                    be2s_tok = t2
            else:
                be2s_tok = None
            with nc.psum_tensor("l2A", [128, 512], F32) as l2A, \
                 nc.psum_tensor("l2B", [128, 512], F32) as l2B, \
                 nc.psum_tensor("logits_ps", [128, NTS * 12], F32) as logits_ps:
                stat2_v = []; stat2_s = []
                for ts in range(NTS):
                    xs = x_tm.ap()[:, ts * D:(ts + 1) * D]
                    if stage >= 2.515:
                        stat2_v.append(P.emit("vector",
                            lambda e, xs=xs, ts=ts: e.reduce_sum(s1c.ap()[:, ts:ts + 1], xs, axis=AX.X),
                            waits=[xadd_tok[ts]], inc=sm_v))
                    else:
                        stat2_v.append(xadd_tok[ts])
                    if stage >= 2.518:
                        stat2_s.append(P.emit("scalar",
                            lambda e, xs=xs, ts=ts: e.activation(sq_dump.ap(), xs, AF.Square,
                                                                 accum_out=s2c.ap()[:, ts:ts + 1]),
                            waits=[xadd_tok[ts]], inc=sm_s))
                    else:
                        stat2_s.append(xadd_tok[ts])
                upto_map = {2.505: 0, 2.508: 0, 2.5085: 0, 2.51: 0, 2.515: 0, 2.518: 0, 2.52: 1, 2.55: 2, 2.58: 3}
                ln2_upto = upto_map.get(stage, 4)
                use_rt = stage >= 2.7
                logits_done = ln_chain(P, x_tm, cols, cb1_rep, sb_rep, h_stage2,
                                       (l2A, l2B), id32, h_fm, hfm2_tok,
                                       stat2_v, stat2_s, reps_tok, sm_t, sm_v, sm_s,
                                       epscol_sb,
                                       rt_f32=(rt_f32 if use_rt else None),
                                       wr_sb=wr_sb, logits_ps=logits_ps,
                                       upto=ln2_upto)

                wt_tok = None
                for ts in (range(NTS) if stage >= 2.9 else ()):
                    P.emit("vector",
                        lambda e, ts=ts: e.tensor_tensor(
                            rl.ap(), logits_ps.ap()[:, ts * 12:(ts + 1) * 12],
                            br_sb.ap(), ALU.add),
                        waits=[logits_done])
                    P.drain("vector")
                    P.emit("vector", lambda e: e.reduce_max(rm1.ap(), rl.ap(), axis=AX.X))
                    P.drain("vector")
                    P.emit("vector",
                           lambda e: e.tensor_scalar(rm1n.ap(), rm1.ap(), -1.0 / TAU, None, ALU.mult))
                    P.emit("vector",
                           lambda e: e.tensor_scalar(req.ap(), rl.ap(), rm1.ap(), None, ALU.is_equal))
                    P.drain("vector")
                    P.emit("vector",
                           lambda e: e.scalar_tensor_tensor(rl2.ap(), req.ap(), -1e30, rl.ap(),
                                                            ALU.mult, ALU.add))
                    P.drain("vector")
                    P.emit("vector", lambda e: e.reduce_max(rm2.ap(), rl2.ap(), axis=AX.X))
                    P.drain("vector")
                    mk = P.emit("vector",
                        lambda e: e.tensor_scalar(rmk.ap(), rl.ap(), rm2.ap(), None, ALU.is_ge),
                        inc=sm_v)
                    ex = P.emit("scalar",
                        lambda e: e.activation(rex.ap(), rl.ap(), AF.Exp,
                                               bias=rm1n.ap(), scale=1.0 / TAU),
                        waits=[mk], inc=sm_s)
                    P.emit("vector",
                           lambda e: e.tensor_tensor(rp.ap(), rex.ap(), rmk.ap(), ALU.mult),
                           waits=[ex])
                    P.drain("vector")
                    P.emit("vector", lambda e: e.reduce_sum(rZs.ap(), rp.ap(), axis=AX.X))
                    P.drain("vector")
                    P.emit("vector", lambda e: e.reciprocal(rrZ.ap(), rZs.ap()))
                    P.drain("vector")
                    wv = P.emit("vector",
                        lambda e, ts=ts: e.tensor_scalar(
                            w_rt.ap()[:, ts * 12:(ts + 1) * 12], rp.ap(), rrZ.ap(),
                            None, ALU.mult),
                        inc=sm_v)
                    P.drain("vector")
                    wtm = P.emit("tensor",
                        lambda e, ts=ts: e.transpose(
                            l2A.ap()[0:12, 0:128],
                            w_rt.ap()[:, ts * 12:(ts + 1) * 12], id32.ap()),
                        waits=[wv, wt_tok], inc=sm_t)
                    wt_tok = P.emit("vector",
                        lambda e, ts=ts: e.tensor_copy(
                            wT.ap()[:, ts * 128:(ts + 1) * 128], l2A.ap()[0:12, 0:128]),
                        waits=[wtm], inc=sm_v)
                routing_tok = wt_tok if stage >= 2.9 else logits_done

        if stage <= 3.59:
            partial_store(routing_tok)
            mid.close()
            finish()
            return nc

        mid.close()

        with ExitStack() as ffn:
            fsb = lambda name, shape, dt=BF16: sballoc(ffn, name, shape, dt)
            act = fsb("act", (128, 24 * TT))
            T_acc = fsb("T_acc", (128, NTSH * D))
            W1buf = [fsb(f"W1buf{i}", (128, 8 * 256)) for i in range(2)]
            W2buf = [fsb(f"W2buf{i}", (128, 24 * 128)) for i in range(2)]
            yE0 = fsb("yE_sb0", (128, 4 * TT))
            yE_sb = [yE0, yE0]
            gg_sb = [fsb(f"gg{i}", (128, TT)) for i in range(2)]

            def npairs(e):
                return 16 if e == 0 else 24

            def nkc2(e):
                return 16 if e == 0 else 24

            hfm_all = hfm2_tok[NTS - 1]
            final_tok_prev = [None] * NTSH

            with nc.psum_tensor("Pg", [128, TT], F32) as Pg, \
                 nc.psum_tensor("Pa", [128, TT], F32) as Pa, \
                 nc.psum_tensor("yE_ps", [128, TT], F32) as yE_ps, \
                 nc.psum_tensor("tr_psA", [128, 1024], BF16) as tr_psA, \
                 nc.psum_tensor("tr_psB", [128, 1024], BF16) as tr_psB:
                tr_ps2 = [tr_psA, tr_psB]

                w1_cnt = [0, 0]
                w2_cnt = [0, 0]
                w1_slot_use = [None, None]
                w2_slot_use = [None, None]
                gelu_tok = None
                prev_stt = None
                yE_ev_v = None
                yE_ev_s = None
                yslot_free = [None, None]
                trhalf_tok = [None, None]

                for th in range(2):
                    toff = th * TT
                    tinit = [None] * NTSH
                    prev_cp = None
                    for ts in range(NTSH):
                        gts = th * NTSH + ts
                        for n2 in range(2):
                            P.emit("tensor",
                                lambda e, gts=gts, n2=n2: e.matmul(
                                    yE_ps.ap()[:, n2 * 512:(n2 + 1) * 512],
                                    wT.ap()[:, gts * 128:(gts + 1) * 128],
                                    be2s_bf.ap()[0:12, n2 * 512:(n2 + 1) * 512],
                                    start=True, stop=True),
                                waits=[routing_tok, be2s_tok, prev_cp, yE_ev_v, yE_ev_s],
                                inc=(sm_t if n2 == 1 else None))
                        mm = sm_t.tok()
                        prev_cp = P.emit("scalar",
                            lambda e, ts=ts: e.copy(
                                T_acc.ap()[:, ts * D:(ts + 1) * D], yE_ps.ap()),
                            waits=[mm, final_tok_prev[ts]], inc=sm_s)
                        tinit[ts] = prev_cp
                    yep_free = prev_cp

                    w1_plan = [(e, pr) for e in range(13) for pr in range(npairs(e))]
                    w2_plan = [(e, fs) for e in range(13) for fs in range(8)]
                    w1_tok = {}
                    w2_tok = {}
                    w1_i = [0]
                    w2_i = [0]

                    def w1_issue():
                        if w1_i[0] >= len(w1_plan):
                            return
                        e, pr = w1_plan[w1_i[0]]
                        sl = w1_i[0] % 2
                        w1_cnt[sl] += 1
                        src = (di["ws1"][:, pr * 256:(pr + 1) * 256] if e == 0
                               else di["we1"][e - 1, :, pr * 256:(pr + 1) * 256])
                        gld(W1buf[sl].ap().rearrange("p (k c) -> p k c", k=8),
                            src.rearrange("(k p) c -> p k c", p=128),
                            sm_w1[sl], waits=[w1_slot_use[sl], routing_tok])
                        w1_tok[(e, pr)] = ((sm_w1[sl], 16 * w1_cnt[sl]), sl)
                        w1_i[0] += 1

                    def w2_issue():
                        if w2_i[0] >= len(w2_plan):
                            return
                        e, fs = w2_plan[w2_i[0]]
                        sl = w2_i[0] % 2
                        kk = nkc2(e)
                        src = (di["ws2"][:, fs * 128:(fs + 1) * 128] if e == 0
                               else di["we2"][e - 1, :, fs * 128:(fs + 1) * 128])
                        src3 = src.rearrange("(k p) c -> p k c", p=128)
                        gate = [w2_slot_use[sl], routing_tok]
                        for k0 in range(0, kk, 8):
                            w2_cnt[sl] += 1
                            gld(W2buf[sl].ap()[:, k0 * 128:(k0 + 8) * 128]
                                .rearrange("p (k c) -> p k c", k=8),
                                src3[:, k0:k0 + 8, :],
                                sm_w2[sl], waits=gate)
                            gate = []
                        w2_tok[(e, fs)] = ((sm_w2[sl], 16 * w2_cnt[sl]), sl)
                        w2_i[0] += 1

                    w1_issue(); w1_issue()
                    w2_issue(); w2_issue()

                    for e in range(13 if stage >= 3.4 else 0):
                        np_ = npairs(e)
                        kk = nkc2(e)
                        stt_last = None
                        for pr in range(np_):
                            dtok, sl = w1_tok[(e, pr)]
                            par = pr % 2
                            first_w = [dtok, hfm_all, gelu_tok]
                            for kc in range(NCH):
                                for tt2 in range(2):
                                    P.emit("tensor",
                                        lambda e_, sl=sl, kc=kc, tt2=tt2, toff=toff: e_.matmul(
                                            Pg.ap()[:, tt2 * 512:(tt2 + 1) * 512],
                                            W1buf[sl].ap()[:, kc * 256 + 128: kc * 256 + 256],
                                            h_fm.ap()[:, kc * T + toff + tt2 * 512:
                                                      kc * T + toff + (tt2 + 1) * 512],
                                            start=(kc == 0), stop=(kc == NCH - 1)),
                                        waits=first_w,
                                        inc=(sm_t if (kc == NCH - 1 and tt2 == 1) else None))
                                    first_w = []
                            gmm = sm_t.tok()
                            bcol = (bs1c_sb.ap()[:, 2 * pr + 1: 2 * pr + 2] if e == 0
                                    else be1c_sb.ap()[:, (e - 1) * 48 + 2 * pr + 1:
                                                      (e - 1) * 48 + 2 * pr + 2])
                            gelu_tok = P.emit("scalar",
                                lambda e_, par=par, bcol=bcol: e_.activation(
                                    gg_sb[par].ap(), Pg.ap(), AF.Gelu, bias=bcol),
                                waits=[gmm, be1c_tok], inc=sm_s)
                            first_w = [prev_stt]
                            for kc in range(NCH):
                                for tt2 in range(2):
                                    P.emit("tensor",
                                        lambda e_, sl=sl, kc=kc, tt2=tt2, toff=toff: e_.matmul(
                                            Pa.ap()[:, tt2 * 512:(tt2 + 1) * 512],
                                            W1buf[sl].ap()[:, kc * 256: kc * 256 + 128],
                                            h_fm.ap()[:, kc * T + toff + tt2 * 512:
                                                      kc * T + toff + (tt2 + 1) * 512],
                                            start=(kc == 0), stop=(kc == NCH - 1)),
                                        waits=first_w,
                                        inc=(sm_t if (kc == NCH - 1 and tt2 == 1) else None))
                                    first_w = []
                            amm = sm_t.tok()
                            w1_slot_use[sl] = amm
                            acol = (bs1c_sb.ap()[:, 2 * pr: 2 * pr + 1] if e == 0
                                    else be1c_sb.ap()[:, (e - 1) * 48 + 2 * pr:
                                                      (e - 1) * 48 + 2 * pr + 1])
                            prev_stt = P.emit("vector",
                                lambda e_, pr=pr, par=par, acol=acol: e_.scalar_tensor_tensor(
                                    act.ap()[:, pr * TT:(pr + 1) * TT],
                                    Pa.ap(), acol, gg_sb[par].ap(),
                                    ALU.add, ALU.mult),
                                waits=[gelu_tok, amm], inc=sm_v)
                            stt_last = prev_stt
                            w1_issue()
                        for fs in range(8 if stage >= 3.6 else 0):
                            dtok2, sl2 = w2_tok[(e, fs)]
                            yslot = (fs // 4) % 2
                            first_w = [dtok2, yE_ev_v, yE_ev_s, stt_last, yep_free]
                            if fs % 4 == 0:
                                first_w.append(yslot_free[yslot])
                            for kc in range(kk):
                                for tt2 in range(2):
                                    P.emit("tensor",
                                        lambda e_, sl2=sl2, kc=kc, tt2=tt2: e_.matmul(
                                            yE_ps.ap()[:, tt2 * 512:(tt2 + 1) * 512],
                                            W2buf[sl2].ap()[:, kc * 128:(kc + 1) * 128],
                                            act.ap()[:, kc * TT + tt2 * 512:
                                                     kc * TT + (tt2 + 1) * 512],
                                            start=(kc == 0), stop=(kc == kk - 1)),
                                        waits=first_w,
                                        inc=(sm_t if (kc == kk - 1 and tt2 == 1) else None))
                                    first_w = []
                            ymm2 = sm_t.tok()
                            w2_slot_use[sl2] = ymm2
                            yE_ev_v = P.emit("vector",
                                lambda e_, yslot=yslot, fs=fs: e_.tensor_copy(
                                    yE_sb[yslot].ap()[:, (fs % 4) * TT: (fs % 4) * TT + 512],
                                    yE_ps.ap()[:, 0:512]),
                                waits=[ymm2], inc=sm_v)
                            yE_ev_s = P.emit("scalar",
                                lambda e_, yslot=yslot, fs=fs: e_.copy(
                                    yE_sb[yslot].ap()[:, (fs % 4) * TT + 512: (fs % 4 + 1) * TT],
                                    yE_ps.ap()[:, 512:1024]),
                                waits=[ymm2], inc=sm_s)
                            w2_issue()
                            if fs % 4 == 3 and stage >= 3.8:
                                fsb_ = fs // 4
                                last_stt = None
                                for ts in range(NTSH):
                                    gts = th * NTSH + ts
                                    half = ts % 2
                                    tw = [yE_ev_v, yE_ev_s, trhalf_tok[half]]
                                    for i in range(4):
                                        P.emit("tensor",
                                            lambda e_, yslot=yslot, i=i, ts=ts, half=half: e_.transpose(
                                                tr_ps2[half].ap()[:, i * 128:(i + 1) * 128],
                                                yE_sb[yslot].ap()[:, i * TT + ts * 128:
                                                                  i * TT + (ts + 1) * 128],
                                                idbf.ap()),
                                            waits=tw, inc=(sm_t if i == 3 else None))
                                        tw = []
                                    trk = sm_t.tok()
                                    if stage < 3.82:
                                        last_stt = P.emit("vector",
                                            lambda e_, half=half: e_.tensor_copy(
                                                gg_sb[0].ap()[:, 0:512],
                                                tr_ps2[half].ap()[:, 0:512]),
                                            waits=[trk], inc=sm_v)
                                        trhalf_tok[half] = last_stt
                                        continue
                                    wcol = (onescol_sb.ap() if e == 0
                                            else w_rt.ap()[:, gts * 12 + (e - 1): gts * 12 + e])
                                    last_stt = P.emit("vector",
                                        lambda e_, ts=ts, fsb_=fsb_, half=half, wcol=wcol:
                                        e_.scalar_tensor_tensor(
                                            T_acc.ap()[:, ts * D + fsb_ * 512:
                                                       ts * D + (fsb_ + 1) * 512],
                                            tr_ps2[half].ap()[:, 0:512],
                                            wcol,
                                            T_acc.ap()[:, ts * D + fsb_ * 512:
                                                       ts * D + (fsb_ + 1) * 512],
                                            ALU.mult, ALU.add),
                                        waits=[trk, tinit[ts]], inc=sm_v)
                                    trhalf_tok[half] = last_stt
                                yslot_free[0] = last_stt
                                yslot_free[1] = last_stt

                    for ts in range(NTSH if stage >= 3.9 else 0):
                        gts = th * NTSH + ts
                        P.emit("vector",
                            lambda e_, ts=ts: e_.tensor_tensor(
                                gg_sb[0].ap(), T_acc.ap()[:, ts * D:(ts + 1) * D],
                                gb_rep.ap(), ALU.mult))
                        P.drain("vector")
                        ftok = P.emit("vector",
                            lambda e_, gts=gts: e_.tensor_tensor(
                                x_tm.ap()[:, gts * D:(gts + 1) * D],
                                x_tm.ap()[:, gts * D:(gts + 1) * D],
                                gg_sb[0].ap(), ALU.add),
                            inc=sm_v)
                        final_tok_prev[ts] = ftok
                        P.emit("sync",
                            lambda e_, gts=gts: e_.dma_start(
                                out_d[gts * 128:(gts + 1) * 128, :],
                                x_tm.ap()[:, gts * D:(gts + 1) * D]),
                            waits=[ftok], inc=sm_st, inck=16)
                    gelu_tok = final_tok_prev[NTSH - 1]

        if stage >= 3.9:
            P.emit("sync", lambda e: None, waits=[(sm_st, 16 * NTS)])
        else:
            partial_store((sm_v, sm_v.n))
            for eng, sem in (("sync", sm_t), ("sync", sm_s)):
                P.emit(eng, lambda e: None, waits=[(sem, sem.n)])
        finish()
    return nc


_CACHE = {}


def _pair_permute_cols(w):
    shp = w.shape
    n2 = shp[-1]
    npr = n2 // 256
    w2 = w.reshape(shp[:-1] + (2, npr, 128))
    w3 = np.moveaxis(w2, -3, -2)
    return np.ascontiguousarray(w3.reshape(shp))


def kernel(L0_attn_x, L0_attn_W_dq, L0_attn_W_uq, L0_attn_W_dkv, L0_attn_W_uk,
           L0_attn_W_uv, L0_attn_W_o, L0_ffn_Ws1, L0_ffn_bs1, L0_ffn_Ws2,
           L0_ffn_bs2, L0_ffn_We1, L0_ffn_be1, L0_ffn_We2, L0_ffn_be2,
           L0_ffn_Wr, L0_ffn_br, L0_c, L0_wmod):
    f = np.float32
    asc = np.ascontiguousarray
    x = np.asarray(L0_attn_x, dtype=f)

    ws1p = _pair_permute_cols(np.asarray(L0_ffn_Ws1, f))
    we1p = _pair_permute_cols(np.asarray(L0_ffn_We1, f))
    bs1p = _pair_permute_cols(np.asarray(L0_ffn_bs1, f).reshape(1, -1))[0]
    be1p = _pair_permute_cols(np.asarray(L0_ffn_be1, f))

    mask = np.zeros((128, 128), f)
    mask[np.triu_indices(128, 1)] = -1e30

    common = {
        "w_dq": asc(np.asarray(L0_attn_W_dq, f)),
        "w_dqT": asc(np.asarray(L0_attn_W_dq, f).T),
        "w_uq": asc(np.asarray(L0_attn_W_uq, f)),
        "w_uqT": asc(np.asarray(L0_attn_W_uq, f).T),
        "w_dkvT": asc(np.asarray(L0_attn_W_dkv, f).T),
        "w_uk": asc(np.asarray(L0_attn_W_uk, f)),
        "w_uv": asc(np.asarray(L0_attn_W_uv, f)),
        "w_oT": asc(np.asarray(L0_attn_W_o, f).T),
        "wmod": asc(np.asarray(L0_wmod, f)),
        "ws1": ws1p,
        "ws2": asc(np.asarray(L0_ffn_Ws2, f)),
        "we1": we1p,
        "we2": asc(np.asarray(L0_ffn_We2, f)),
        "wr": asc(np.asarray(L0_ffn_Wr, f)),
        "br_rep": asc(np.broadcast_to(np.asarray(L0_ffn_br, f), (128, 12))),
        "bs1c": asc(bs1p.reshape(32, 128).T),
        "be1c": asc(be1p.reshape(NE, 48, 128).transpose(2, 0, 1)),
        "be2p": np.concatenate([np.asarray(L0_ffn_be2, f),
                                np.zeros((128 - NE, 1024), f)], axis=0),
        "bs2rp": asc(np.broadcast_to(np.asarray(L0_ffn_bs2, f), (128, 1024))),
        "ones1": np.ones((1, 128), f),
        "onescol": np.ones((128, 1), f),
        "ident32": np.eye(128, dtype=f),
        "epscol": np.full((128, 1), 1e-5, dtype=f),
        "identbf": np.eye(128, dtype=ml_dtypes.bfloat16),
        "maskbf": mask.astype(ml_dtypes.bfloat16),
    }
    c = np.asarray(L0_c, f)
    in_maps = []
    for b in range(8):
        m = dict(common)
        m["x"] = asc(x[b])
        m["c_cols"] = asc(c[b].reshape(8, 128).T)
        in_maps.append(m)

    import os as _os
    _stage = float(_os.environ.get("MK_STAGE", "4"))
    if _CACHE.get("stage") != _stage:
        _CACHE["nc"] = build_program(_stage)
        _CACHE["stage"] = _stage
    nc = _CACHE["nc"]
    res = run_bass_kernel_spmd(nc, in_maps, core_ids=list(range(8)))
    return np.stack([res.results[b]["out"] for b in range(8)], axis=0)
